# version 21
# baseline (speedup 1.0000x reference)
import numpy as np
import ml_dtypes
import concourse.bass as bass
import concourse.mybir as mybir
from concourse.bass_utils import run_bass_kernel_spmd
from contextlib import ExitStack

F32 = mybir.dt.float32
BF16 = mybir.dt.bfloat16
AF = mybir.ActivationFunctionType
ALU = mybir.AluOpType
AX = mybir.AxisListType

NCORES = 8
D = 2048
T = 1024
KT = D // 128
E = 8
G = 4
NBUF = 6
EPS = 1e-5


class Sem:
    def __init__(self, nc, name):
        self.name = name
        self.h = nc.alloc_semaphore(name=name)
        self.count = 0


class Emitter:

    def __init__(self):
        self.q = {"sync": [], "gpsimd": [], "vector": [], "scalar": [], "tensor": []}
        self.waited = {k: {} for k in self.q}
        self.frozen = False

    def op(self, eng, fn, waits=(), incs=()):
        if self.frozen:
            for s, n in incs:
                s.count += n
            return
        real_waits = []
        for s, v in waits:
            assert v <= s.count, f"wait on future inc: {s.name} {v} > {s.count}"
            if self.waited[eng].get(s.name, -1) < v:
                self.waited[eng][s.name] = v
                real_waits.append((s, v))
        for s, n in incs:
            s.count += n
        self.q[eng].append((fn, real_waits, list(incs)))

    def play(self, engine, name):
        for fn, waits, incs in self.q[name]:
            for s, v in waits:
                engine.wait_ge(s.h, v)
            ins = fn(engine)
            assert len(incs) <= 1, "nop-incs are unsafe (fire before engine-queue drain)"
            for s, n in incs:
                ins.then_inc(s.h, n)
            if name == "vector":
                engine.nop()


def build_program(dbg=None):
    nc = bass.Bass("TRN2", target_bir_lowering=False)

    def din(name, shape, dt=F32):
        return nc.dram_tensor(name, shape, dt, kind="ExternalInput").ap()

    xt_h = din("xt", [128, KT * T])
    wq_h = din("wq", [KT, 128, D])
    wkvd_h = din("wkvd", [2 * G, 128, D])
    w1_h = din("w1", [E, KT, 128, D])
    w2_h = din("w2", [E, KT, 128, D])
    gate_h = din("gatew", [128, KT * E])
    b1_h = din("b1", [128, E * KT])
    b2_h = din("b2", [E, D])
    bq_h = din("bq", [128, KT])
    bkvd_h = din("bkvd", [128, 2 * G])
    alpha_h = din("alpha", [128, 8 * E])
    maskA_h = din("maskA", [128, KT * 32], BF16)
    maskT_h = din("maskT", [32, KT * 128], BF16)
    onesm_h = din("onesm", [128, 128], BF16)
    ident_h = din("ident", [128, 128])
    sel_h = din("sel", [E, E * 128], BF16)
    yt_h = nc.dram_tensor("yt", [128, KT * T], F32, kind="ExternalOutput").ap()

    with ExitStack() as ctx:
        def sb(name, shape, dt=F32):
            return ctx.enter_context(nc.sbuf_tensor(name, shape, dt))

        def psum(name, shape):
            return ctx.enter_context(nc.psum_tensor(name, shape, F32))

        xres = sb("xres", [128, KT * T])
        xln = sb("xln", [128, KT * T], BF16)
        xg = sb("xg", [128, KT * T], BF16)
        wbuf = [sb(f"wbuf{i}", [128, D], BF16) for i in range(NBUF)]
        gbuf = [sb(f"gbuf{i}", [128, T]) for i in range(4)]
        cb = [sb(f"cb{i}", [128, T]) for i in range(2)]
        b2bf = sb("b2bf", [E, D], BF16)
        gatebf = sb("gatebf", [128, KT * E], BF16)
        b1_sb = sb("b1_sb", [128, E * KT])
        bq_sb = sb("bq_sb", [128, KT])
        bkvd_sb = sb("bkvd_sb", [128, 2 * G])
        alpha_sb = sb("alpha_sb", [128, 8 * E])
        maskA_sb = sb("maskA_sb", [128, KT * 32], BF16)
        maskT_sb = sb("maskT_sb", [32, KT * 128], BF16)
        onesm_sb = sb("onesm_sb", [128, 128], BF16)
        ident_sb = sb("ident_sb", [128, 128])
        sel_sb = sb("sel_sb", [E, E * 128], BF16)
        s_sb = gbuf[2][0:32, :]
        d_sb = gbuf[3][0:32, :]
        attn_bf = sb("attn_bf", [32, T], BF16)
        logits = sb("logits", [128, 8 * E])
        comb = sb("comb", [128, 8 * E])
        combT = sb("combT", [E, T])
        combTbf = sb("combTbf", [E, T], BF16)
        gEX = sb("gEX", [128, 64])
        gP8 = sb("gP8", [128, 64])
        gM8 = sb("gM8", [128, 64])
        gPM = sb("gPM", [128, 64])
        gEQ = sb("gEQ", [128, 64])
        gC = sb("gC", [128, 64])
        gROW = sb("gROW", [128, 72])
        vdup = xg[:, 0:G * T]
        kdup = xg[:, G * T:2 * G * T]
        bfA = [xg[:, 8 * T + i * T: 8 * T + (i + 1) * T] for i in range(2)]
        bfB = [xg[:, 10 * T + i * T: 10 * T + (i + 1) * T] for i in range(2)]
        rstd_bf = xg[:, 12 * T:13 * T]
        sub_bf = [xg[:, 13 * T + i * T: 13 * T + (i + 1) * T] for i in range(2)]
        ae = [xg[:, 15 * T:16 * T], xg[:, 4 * T:5 * T]]

        pA = [psum(f"pA{i}", [128, T]) for i in range(2)]
        pB = [psum(f"pB{i}", [128, T]) for i in range(2)]

        s_xc = [Sem(nc, f"s_x{i}") for i in range(KT)]
        s_c = Sem(nc, "s_c")
        s_c0 = Sem(nc, "s_c0")
        s_cg = Sem(nc, "s_cg")
        s_wslot = [Sem(nc, f"s_w{i}") for i in range(NBUF)]
        s_wf = Sem(nc, "s_wf")
        s_t = Sem(nc, "s_t")
        s_s = Sem(nc, "s_s")
        s_v = Sem(nc, "s_v")
        s_g = Sem(nc, "s_g")
        s_st = Sem(nc, "s_st")

        em = Emitter()
        block = ctx.enter_context(nc.Block())

        def xchunk(ft):
            em.op("sync", lambda e, ft=ft: e.dma_start(
                out=xres[:, ft * T:(ft + 1) * T],
                in_=xt_h[:, ft * T:(ft + 1) * T]),
                  incs=[(s_xc[ft], 16)])

        em.op("sync", lambda e: e.dma_start(out=onesm_sb[:], in_=onesm_h[:]),
              incs=[(s_c0, 16)])
        xchunk(0)
        xchunk(1)
        for dst, src in [(bq_sb[:], bq_h[:]), (bkvd_sb[:], bkvd_h[:]),
                         (b1_sb[:], b1_h[:]), (alpha_sb[:], alpha_h[:]),
                         (maskA_sb[:], maskA_h[:]), (maskT_sb[:], maskT_h[:]),
                         (ident_sb[:], ident_h[:]),
                         (sel_sb[:], sel_h[:])]:
            em.op("sync", lambda e, d=dst, s=src: e.dma_start(out=d, in_=s),
                  incs=[(s_c, 16)])
        IN_ALL = s_c.count
        for ft in range(2, KT):
            xchunk(ft)

        em.op("gpsimd", lambda e: e.dma_start(out=gatebf[:], in_=gate_h[:]),
              incs=[(s_cg, 16)])
        em.op("gpsimd", lambda e: e.dma_start(out=b2bf[:], in_=b2_h[:]),
              incs=[(s_cg, 16)])
        CG_ALL = s_cg.count

        strip_srcs = []
        for g in range(G):
            strip_srcs.append(wkvd_h[2 * g])
        for qt in range(KT):
            strip_srcs.append(wq_h[qt])
        for g in range(G):
            strip_srcs.append(wkvd_h[2 * g + 1])
        for e in range(E):
            for it in range(KT):
                strip_srcs.append(w1_h[e, it])
            for ot in range(KT):
                strip_srcs.append(w2_h[e, ot])
        NSTRIP = len(strip_srcs)
        strip_ready = [None] * NSTRIP
        next_strip = [0]

        def issue_strip(free_wait=None):
            s = next_strip[0]
            if s >= NSTRIP:
                return
            next_strip[0] = s + 1
            waits = []
            if s >= NBUF:
                assert free_wait is not None
                waits.append(free_wait)
            if s == 0:
                waits.append((s_xc[KT - 1], 16))
            slot = s % NBUF
            em.op("gpsimd",
                  lambda e, d=wbuf[slot][:], src=strip_srcs[s]: e.dma_start(out=d, in_=src),
                  waits=waits, incs=[(s_wslot[slot], 16)])
            strip_ready[s] = (s_wslot[slot], s_wslot[slot].count)

        for _ in range(NBUF):
            issue_strip()

        def mm(pt, lhsT, rhs, start, stop, waits=(), incs=()):
            em.op("tensor",
                  lambda e, o=pt, l=lhsT, r=rhs, st_=start, sp=stop:
                      e.matmul(o, l, r, start=st_, stop=sp),
                  waits=waits, incs=incs)

        def layernorm(dst_bf, ready):
            copy_inc = [0] * KT
            sq_inc = [0] * KT
            ft_mm = [0] * KT
            for ft in range(KT):
                w = [ready[ft]]
                if ft >= 2:
                    w.append((s_t, ft_mm[ft - 2]))
                em.op("vector",
                      lambda e, d=bfA[ft % 2], s=xres[:, ft * T:(ft + 1) * T]:
                          e.tensor_copy(d, s),
                      waits=w, incs=[(s_v, 1)])
                copy_inc[ft] = s_v.count
                em.op("scalar",
                      lambda e, d=bfB[ft % 2], s=xres[:, ft * T:(ft + 1) * T]:
                          e.square(d, s),
                      waits=w, incs=[(s_s, 1)])
                sq_inc[ft] = s_s.count
                for h in range(2):
                    mm(pB[0][:, h * 512:(h + 1) * 512], onesm_sb[:],
                       bfA[ft % 2][:, h * 512:(h + 1) * 512],
                       start=(ft == 0), stop=(ft == KT - 1),
                       waits=[(s_v, copy_inc[ft]), (s_c0, 16)])
                for h in range(2):
                    inc = [(s_t, 1)] if (h == 1) else []
                    mm(pB[1][:, h * 512:(h + 1) * 512], onesm_sb[:],
                       bfB[ft % 2][:, h * 512:(h + 1) * 512],
                       start=(ft == 0), stop=(ft == KT - 1),
                       waits=[(s_s, sq_inc[ft])], incs=inc)
                ft_mm[ft] = s_t.count
            em.op("vector", lambda e: e.tensor_copy(gbuf[0][:], pB[0][:]),
                  waits=[(s_t, ft_mm[KT - 1])])
            em.op("scalar", lambda e: e.copy(gbuf[1][:], pB[1][:]),
                  waits=[(s_t, ft_mm[KT - 1])], incs=[(s_s, 1)])
            stats_evac = s_s.count
            em.op("vector", lambda e: e.tensor_scalar(cb[0][:], gbuf[0][:], 1.0 / D, None, op0=ALU.mult),
                  incs=[(s_v, 1)])
            m_done = s_v.count
            rstd_h = [0, 0]
            for hh in range(2):
                c = slice(hh * 512, (hh + 1) * 512)
                em.op("vector",
                      lambda e, c=c: e.tensor_scalar(gbuf[1][:, c], gbuf[1][:, c], 1.0 / D, EPS, op0=ALU.mult, op1=ALU.add),
                      waits=([(s_s, stats_evac)] if hh == 0 else []))
                em.op("vector",
                      lambda e, c=c: e.tensor_tensor(gbuf[0][:, c], cb[0][:, c], cb[0][:, c], op=ALU.mult))
                em.op("vector",
                      lambda e, c=c: e.tensor_tensor(gbuf[1][:, c], gbuf[1][:, c], gbuf[0][:, c], op=ALU.subtract),
                      incs=[(s_v, 1)])
                vh = s_v.count
                em.op("scalar",
                      lambda e, c=c: e.activation(gbuf[0][:, c], gbuf[1][:, c], AF.Ln, bias=0.0, scale=1.0),
                      waits=[(s_v, vh)])
                em.op("scalar",
                      lambda e, c=c: e.activation(rstd_bf[:, c], gbuf[0][:, c], AF.Exp, bias=0.0, scale=-0.5),
                      incs=[(s_s, 1)])
                rstd_h[hh] = s_s.count
            rstd_done = rstd_h[1]
            apply_inc = [0] * KT
            h0_inc = 0
            for ft in range(KT):
                w = [(s_v, m_done), ready[ft]]
                if ft >= 2:
                    w.append((s_v, apply_inc[ft - 2]))
                em.op("vector",
                      lambda e, ft=ft: e.tensor_tensor(
                          sub_bf[ft % 2], xres[:, ft * T:(ft + 1) * T],
                          cb[0][:], op=ALU.subtract),
                      waits=w)
                if ft == 0:
                    em.op("vector",
                          lambda e, d=dst_bf: e.tensor_tensor(
                              d[:, 0:512], sub_bf[0][:, 0:512],
                              rstd_bf[:, 0:512], op=ALU.mult),
                          waits=[(s_s, rstd_h[0])], incs=[(s_v, 1)])
                    h0_inc = s_v.count
                    em.op("vector",
                          lambda e, d=dst_bf: e.tensor_tensor(
                              d[:, 512:1024], sub_bf[0][:, 512:1024],
                              rstd_bf[:, 512:1024], op=ALU.mult),
                          waits=[(s_s, rstd_h[1])], incs=[(s_v, 1)])
                else:
                    em.op("vector",
                          lambda e, ft=ft, d=dst_bf: e.tensor_tensor(
                              d[:, ft * T:(ft + 1) * T], sub_bf[ft % 2], rstd_bf,
                              op=ALU.mult),
                          waits=[(s_s, rstd_done)], incs=[(s_v, 1)])
                apply_inc[ft] = s_v.count
            return apply_inc, h0_inc

        def dbg_done(ap, cast=False, wait=None):
            q = "gpsimd" if cast else "sync"
            em.op(q, lambda e, a=ap: e.dma_start(out=yt_h[0:ap.shape[0], 0:ap.shape[-1]], in_=a),
                  waits=[wait] if wait else [], incs=[(s_st, 16)])
            em.op("sync", lambda e: e.nop(), waits=[(s_st, 16)])
            em.frozen = True

        if dbg == "io":
            em.op("sync", lambda e: e.dma_start(out=yt_h[:], in_=xres[:]),
                  waits=[(s_xc[i], 16) for i in range(KT)], incs=[(s_st, 16)])
            em.op("sync", lambda e: e.nop(), waits=[(s_st, 16)])
            em.frozen = True

        ln1_inc, ln1_h0 = layernorm(xln, [(s_xc[ft], 16) for ft in range(KT)])
        LN1_ALL = ln1_inc[KT - 1]
        if dbg == "ln1":
            dbg_done(xln[:], cast=True, wait=(s_v, LN1_ALL))

        pa_free = [0, 0]
        pa_free_sem = [s_s, s_s]
        strip_i = 0
        nproj = [0]

        def emit_kv_group(g, is_v, par):
            nonlocal strip_i
            for kt in range(KT):
                for h in range(2):
                    w = [strip_ready[strip_i], (s_v, ln1_inc[kt])]
                    if nproj[0] >= 2:
                        w.append((pa_free_sem[par], pa_free[par]))
                    incs = []
                    if kt == KT - 1 and h == 1:
                        incs = [(s_t, 1)]
                    mm(pA[par][:, h * 512:(h + 1) * 512],
                       wbuf[strip_i % NBUF][:, kt * 128:(kt + 1) * 128],
                       xln[:, kt * T + h * 512: kt * T + (h + 1) * 512],
                       start=(kt == 0), stop=(kt == KT - 1),
                       waits=w, incs=incs)
            grp = s_t.count
            strip_i += 1
            nproj[0] += 1
            dst = (vdup if is_v else kdup)[:, g * T:(g + 1) * T]
            em.op("scalar",
                  lambda e, d=dst, p=pA[par][:], c=bkvd_sb[:, 2 * g + is_v:2 * g + is_v + 1]:
                      e.activation(d, p, AF.Identity, bias=c, scale=1.0),
                  waits=[(s_t, grp), (s_c, IN_ALL)], incs=[(s_s, 1)])
            pa_free[par] = s_s.count
            pa_free_sem[par] = s_s
            issue_strip(free_wait=(s_s, s_s.count))
            return s_s.count

        def emit_kv_quad():
            nonlocal strip_i
            s4 = [strip_i + i for i in range(4)]
            slots = [pA[0], pA[1], pB[0], pB[1]]
            for kt in range(KT):
                for gp in range(4):
                    for h in range(2):
                        lnw = ln1_h0 if (kt == 0 and h == 0) else ln1_inc[kt]
                        w = [strip_ready[s4[gp]], (s_v, lnw)]
                        incs = []
                        if kt == KT - 1 and h == 1:
                            incs = [(s_t, 1)]
                        mm(slots[gp][:, h * 512:(h + 1) * 512],
                           wbuf[s4[gp] % NBUF][:, kt * 128:(kt + 1) * 128],
                           xln[:, kt * T + h * 512: kt * T + (h + 1) * 512],
                           start=(kt == 0), stop=(kt == KT - 1),
                           waits=w, incs=incs)
            grps = [s_t.count - 3 + i for i in range(4)]
            strip_i += 4
            nproj[0] += 4
            evs = [0] * 4
            for gp in range(4):
                em.op("scalar",
                      lambda e, d=kdup[:, gp * T:(gp + 1) * T], p=slots[gp][:],
                             c=bkvd_sb[:, 2 * gp:2 * gp + 1]:
                          e.activation(d, p, AF.Identity, bias=c, scale=1.0),
                      waits=[(s_t, grps[gp]), (s_c, IN_ALL)], incs=[(s_s, 1)])
                evs[gp] = s_s.count
                issue_strip(free_wait=(s_s, s_s.count))
            pa_free[0], pa_free[1] = evs[0], evs[1]
            pa_free_sem[0] = pa_free_sem[1] = s_s
            return evs

        kv_evs = emit_kv_quad()

        kdup_ready = s_s.count
        if dbg == "kv":
            dbg_done(xg[:, 0:2 * G * T], cast=True, wait=(s_s, kdup_ready))
        pv_inc = [0] * KT
        scores_grp = [0] * KT
        qslot = [pA[0], pA[1], pB[1]]
        qfree = [(pa_free_sem[0], pa_free[0]), (pa_free_sem[1], pa_free[1]),
                 (s_s, kv_evs[3])]
        for qt in range(KT):
            par = (qt + 1) % 3
            g = qt // 4
            for kt in range(KT):
                for h in range(2):
                    w = [strip_ready[strip_i], (s_v, ln1_inc[kt])]
                    w.append(qfree[par])
                    incs = []
                    if kt == KT - 1 and h == 1:
                        incs = [(s_t, 1)]
                    mm(qslot[par][:, h * 512:(h + 1) * 512],
                       wbuf[strip_i % NBUF][:, kt * 128:(kt + 1) * 128],
                       xln[:, kt * T + h * 512: kt * T + (h + 1) * 512],
                       start=(kt == 0), stop=(kt == KT - 1),
                       waits=w, incs=incs)
            grp = s_t.count
            strip_i += 1

            def emit_scores(j):
                for h in range(2):
                    incs = [(s_t, 1)] if h == 1 else []
                    mm(pB[0][0:32, h * 512:(h + 1) * 512],
                       maskA_sb[:, j * 32:(j + 1) * 32],
                       bfB[j % 2][:, h * 512:(h + 1) * 512],
                       start=(j == 0), stop=(j == KT - 1),
                       waits=[(s_v, pv_inc[j])], incs=incs)
                scores_grp[j] = s_t.count
            if qt >= 1:
                emit_scores(qt - 1)

            em.op("scalar",
                  lambda e, d=bfA[qt % 2][:], p=qslot[par][:], c=bq_sb[:, qt:qt + 1]:
                      e.activation(d, p, AF.Identity, bias=c, scale=1.0),
                  waits=[(s_t, grp)], incs=[(s_s, 1)])
            qfree[par] = (s_s, s_s.count)
            qev = s_s.count
            issue_strip(free_wait=(s_s, qev))
            pw = [(s_s, qev)]
            if qt >= 2:
                pw.append((s_t, scores_grp[qt - 2]))
            em.op("vector",
                  lambda e, d=bfB[qt % 2][:], q=bfA[qt % 2][:], k=kdup[:, g * T:(g + 1) * T]:
                      e.tensor_tensor(d, q, k, op=ALU.mult),
                  waits=pw, incs=[(s_v, 1)])
            pv_inc[qt] = s_v.count
        pa_free = [qfree[0][1], qfree[1][1]]
        pa_free_sem = [qfree[0][0], qfree[1][0]]
        vdup_ev = [0] * G
        vdup_ev[0] = emit_kv_group(0, 1, 0)
        emit_scores(KT - 1)
        SCORES = s_t.count

        em.op("vector", lambda e: e.tensor_scalar(s_sb[:], pB[0][0:32, :], 1.0 / 8.0, None, op0=ALU.mult),
              waits=[(s_t, SCORES)], incs=[(s_v, 1)])
        if dbg == "scores":
            dbg_done(s_sb[:], wait=(s_v, s_v.count))
        em.op("vector", lambda e: e.tensor_tensor(d_sb[:], s_sb[:], s_sb[:], op=ALU.subtract),
              incs=[(s_v, 1)])
        sm_sub = s_v.count
        em.op("scalar", lambda e: e.activation(s_sb[:], d_sb[:], AF.Exp, bias=0.0, scale=1.0),
              waits=[(s_v, sm_sub)])
        em.op("scalar", lambda e: e.activation(d_sb[:], d_sb[:], AF.Exp, bias=0.0, scale=-1.0),
              incs=[(s_s, 1)])
        sm_exp = s_s.count
        em.op("vector", lambda e: e.tensor_tensor(attn_bf[:], s_sb[:], d_sb[:], op=ALU.mult),
              waits=[(s_s, sm_exp)], incs=[(s_v, 1)])
        ATTN = s_v.count

        epi_inc = [0] * KT

        def emit_epilogue(ft):
            par = ft % 2
            g = ft // 4
            w = [(s_v, ATTN)]
            w.append((pa_free_sem[par], pa_free[par]))
            for h in range(2):
                incs = [(s_t, 1)] if h == 1 else []
                mm(pA[par][:, h * 512:(h + 1) * 512],
                   maskT_sb[0:32, ft * 128:(ft + 1) * 128],
                   attn_bf[0:32, h * 512:(h + 1) * 512],
                   start=True, stop=True,
                   waits=w, incs=incs)
            grp = s_t.count
            ew = [(s_t, grp)]
            if ft >= 2:
                ew.append((s_v, epi_inc[ft - 2]))
            em.op("scalar",
                  lambda e, d=ae[par], p=pA[par][:]: e.copy(d, p),
                  waits=ew, incs=[(s_s, 1)])
            pa_free[par] = s_s.count
            pa_free_sem[par] = s_s
            aev = s_s.count
            em.op("vector",
                  lambda e, d=ae[par], v=vdup[:, g * T:(g + 1) * T]:
                      e.tensor_tensor(d, d, v, op=ALU.mult),
                  waits=[(s_s, aev), (s_s, vdup_ev[g])])
            em.op("vector",
                  lambda e, ft=ft, scr=ae[par]: e.tensor_tensor(
                      xres[:, ft * T:(ft + 1) * T], xres[:, ft * T:(ft + 1) * T],
                      scr, op=ALU.add),
                  incs=[(s_v, 1)])
            epi_inc[ft] = s_v.count

        vdup_ev[1] = emit_kv_group(1, 1, 1)
        for ft in range(4):
            emit_epilogue(ft)
        vdup_ev[2] = emit_kv_group(2, 1, 0)
        vdup_ev[3] = emit_kv_group(3, 1, 1)
        for ft in range(4, KT):
            emit_epilogue(ft)

        if dbg == "attn":
            dbg_done(xres[:], wait=(s_v, epi_inc[KT - 1]))

        ln2_inc, ln2_h0 = layernorm(xln, [(s_v, epi_inc[ft]) for ft in range(KT)])
        LN2_ALL = ln2_inc[KT - 1]

        sc = {"i": strip_i}
        pb_free = [None, None]
        xg_inc = [0] * KT
        acc_inc = [0] * KT
        gl_inc = [0] * KT
        cb_ready = [0, 0]

        def emit_fc1_group(e_idx, it):
            par = it % 2
            for kt in range(KT):
                for h in range(2):
                    w = [strip_ready[sc["i"]], (pa_free_sem[par], pa_free[par])]
                    incs = []
                    if kt == KT - 1 and h == 1:
                        incs = [(s_t, 1)]
                    mm(pA[par][:, h * 512:(h + 1) * 512],
                       wbuf[sc["i"] % NBUF][:, kt * 128:(kt + 1) * 128],
                       xln[:, kt * T + h * 512: kt * T + (h + 1) * 512],
                       start=(kt == 0), stop=(kt == KT - 1),
                       waits=w, incs=incs)
            grp = s_t.count
            sc["i"] += 1
            em.op("scalar",
                  lambda e, it=it, p=pA[par][:],
                         c=b1_sb[:, e_idx * KT + it: e_idx * KT + it + 1]:
                      e.activation(xg[:, it * T:(it + 1) * T], p, AF.Gelu, bias=c, scale=1.0),
                  waits=[(s_t, grp)], incs=[(s_s, 1)])
            pa_free[par] = s_s.count
            pa_free_sem[par] = s_s
            gl_inc[it] = s_s.count
            issue_strip(free_wait=(s_s, gl_inc[it]))

        def emit_cscale(e_idx, it):
            em.op("vector",
                  lambda e, it=it, c=cb[e_idx % 2][:]:
                      e.tensor_tensor(xg[:, it * T:(it + 1) * T],
                                      xg[:, it * T:(it + 1) * T], c, op=ALU.mult),
                  waits=[(s_s, gl_inc[it]), (s_s, cb_ready[e_idx % 2])], incs=[(s_v, 1)])
            xg_inc[it] = s_v.count


        def emit_fc1_quad():
            s4 = [sc["i"] + i for i in range(4)]
            slots = [pA[0], pA[1], pB[0], pB[1]]
            for kt in range(KT):
                for itp in range(4):
                    for h in range(2):
                        lnw = ln2_h0 if (kt == 0 and h == 0) else ln2_inc[kt]
                        w = [strip_ready[s4[itp]], (s_v, lnw)]
                        if itp < 2:
                            w.append((pa_free_sem[itp], pa_free[itp]))
                        incs = []
                        if kt == KT - 1 and h == 1:
                            incs = [(s_t, 1)]
                        mm(slots[itp][:, h * 512:(h + 1) * 512],
                           wbuf[s4[itp] % NBUF][:, kt * 128:(kt + 1) * 128],
                           xln[:, kt * T + h * 512: kt * T + (h + 1) * 512],
                           start=(kt == 0), stop=(kt == KT - 1),
                           waits=w, incs=incs)
            grps = [s_t.count - 3 + i for i in range(4)]
            sc["i"] += 4
            for itp in range(4):
                em.op("scalar",
                      lambda e, it=itp, p=slots[itp][:],
                             c=b1_sb[:, itp:itp + 1]:
                          e.activation(xg[:, it * T:(it + 1) * T], p, AF.Gelu, bias=c, scale=1.0),
                      waits=[(s_t, grps[itp])], incs=[(s_s, 1)])
                gl_inc[itp] = s_s.count
                issue_strip(free_wait=(s_s, gl_inc[itp]))
            pa_free[0], pa_free[1] = gl_inc[0], gl_inc[1]
            pa_free_sem[0] = pa_free_sem[1] = s_s

        emit_fc1_quad()

        emit_fc1_group(0, 4)
        for h in range(2):
            for kt in range(KT):
                w = [(s_cg, CG_ALL), (s_v, ln2_inc[kt])]
                if h == 0 and kt == 0:
                    w.append((s_s, gl_inc[3]))
                incs = [(s_t, 1)] if kt == KT - 1 else []
                mm(pB[1][0:8, h * 512:(h + 1) * 512],
                   gatebf[:, kt * 8:(kt + 1) * 8],
                   xln[:, kt * T + h * 512: kt * T + (h + 1) * 512],
                   start=(kt == 0), stop=(kt == KT - 1),
                   waits=w, incs=incs)
        GATE_MM = s_t.count
        ltT = gbuf[2][0:8, :]
        em.op("scalar", lambda e, d=ltT: e.copy(d, pB[1][0:8, :]),
              waits=[(s_t, GATE_MM)], incs=[(s_s, 1)])
        LTT_EV = s_s.count

        emit_fc1_group(0, 5)

        for tt in range(8):
            incs = [(s_t, 1)] if tt == 7 else []
            mm(pB[0][:, tt * 8:(tt + 1) * 8],
               ltT[:, tt * 128:(tt + 1) * 128],
               ident_sb[0:8, 0:8],
               start=True, stop=True,
               waits=[(s_s, LTT_EV)], incs=incs)
        LTR_MM = s_t.count
        em.op("scalar", lambda e: e.copy(logits[:, 0:64], pB[0][:, 0:64]),
              waits=[(s_t, LTR_MM)], incs=[(s_s, 1)])
        LOGITS_EV = s_s.count

        emit_fc1_group(0, 6)

        def T8(tt):
            return slice(tt * 8, (tt + 1) * 8)

        def C1(base, tt):
            return gROW[:, base + tt: base + tt + 1]

        NM, S2c, WS, RWS = 0, 8, 16, 24

        def vstep(fns, waits=()):
            for i, fn in enumerate(fns):
                em.op("vector", fn, waits=(waits if i == 0 else ()),
                      incs=([(s_v, 1)] if i == len(fns) - 1 else []))
            em.op("vector", lambda e: e.engine_nop(), waits=[(s_v, s_v.count)])
            return s_v.count

        vstep([(lambda e, tt=tt: e.reduce_max(C1(NM, tt), logits[:, T8(tt)], axis=AX.X, negate=True))
               for tt in range(8)], waits=[(s_s, LOGITS_EV)])
        nm_done = s_v.count
        for tt in range(8):
            em.op("scalar", lambda e, tt=tt: e.activation(
                gEX[:, T8(tt)], logits[:, T8(tt)], AF.Exp, bias=C1(NM, tt), scale=1.0),
                  waits=[(s_v, nm_done)], incs=([(s_s, 1)] if tt == 7 else []))
        ex_done = s_s.count
        vstep([lambda e: e.tensor_scalar(gM8[:], gEX[:], 1.0, 2.0, op0=ALU.is_ge, op1=ALU.mult)],
              waits=[(s_s, ex_done)])
        vstep([lambda e: e.tensor_tensor(gPM[:], gEX[:], gM8[:], op=ALU.subtract)])
        vstep([(lambda e, tt=tt: e.reduce_max(C1(S2c, tt), gPM[:, T8(tt)], axis=AX.X))
               for tt in range(8)])
        vstep([(lambda e, tt=tt: e.tensor_scalar(gM8[:, T8(tt)], gEX[:, T8(tt)], C1(S2c, tt), None, op0=ALU.is_ge))
               for tt in range(8)])
        vstep([lambda e: e.tensor_scalar(gROW[:, WS:WS + 8], gROW[:, S2c:S2c + 8], 1.0, None, op0=ALU.add)])
        vstep([lambda e: e.reciprocal(gROW[:, RWS:RWS + 8], gROW[:, WS:WS + 8])])
        vstep([lambda e: e.tensor_tensor(gC[:], gEX[:], gM8[:], op=ALU.mult)])
        vstep([(lambda e, tt=tt: e.tensor_scalar(gC[:, T8(tt)], gC[:, T8(tt)], C1(RWS, tt), None, op0=ALU.mult))
               for tt in range(8)])
        vstep([lambda e: e.tensor_tensor(comb[:], gC[:], alpha_sb[:], op=ALU.mult)],
              waits=[(s_c, IN_ALL)])

        COMB = s_v.count
        emit_fc1_group(0, 7)
        tr_ev = 0
        for tt in range(8):
            bank = tt % 2
            w = [(s_v, COMB)]
            if tt >= 2:
                w.append((s_s, tr_ev - 1))
            em.op("tensor",
                  lambda e, tt=tt, off=bank * 512: e.transpose(
                      pB[0][0:8, off:off + 128], comb[:, tt * 8:(tt + 1) * 8], ident_sb[:]),
                  waits=w, incs=[(s_t, 1)])
            grp = s_t.count
            em.op("scalar",
                  lambda e, tt=tt, off=bank * 512: e.copy(
                      combT[0:8, tt * 128:(tt + 1) * 128], pB[0][0:8, off:off + 128]),
                  waits=[(s_t, grp)], incs=[(s_s, 1)])
            tr_ev = s_s.count
        em.op("vector", lambda e: e.tensor_copy(combTbf[:], combT[:]),
              waits=[(s_s, tr_ev)], incs=[(s_v, 1)])
        COMBT = s_v.count
        if dbg == "gate":
            dbg_done(comb[:], wait=(s_v, COMB))
        if dbg == "combT":
            dbg_done(combT[:], wait=(s_s, tr_ev))
        if dbg == "logits":
            dbg_done(logits[:], wait=(s_s, LOGITS_EV))

        emit_fc1_group(0, 8)

        def emit_cb(e_idx, pt, free_wait, extra_evac_wait=None):
            w = [(s_v, COMBT), free_wait]
            for h in range(2):
                incs = [(s_t, 1)] if h == 1 else []
                mm(pt[:, h * 512:(h + 1) * 512],
                   sel_sb[0:E, e_idx * 128:(e_idx + 1) * 128],
                   combTbf[0:E, h * 512:(h + 1) * 512],
                   start=True, stop=True, waits=w, incs=incs)
            grp = s_t.count
            ew = [(s_t, grp)]
            if extra_evac_wait is not None:
                ew.append(extra_evac_wait)
            em.op("scalar",
                  lambda e, d=cb[e_idx % 2][:], p=pt[:]: e.copy(d, p),
                  waits=ew, incs=[(s_s, 1)])
            return s_s.count

        cb_ready[0] = emit_cb(0, pB[0][:], (s_s, tr_ev))
        pb_free[0] = (s_s, cb_ready[0])
        pb_free[1] = (s_s, LTT_EV)
        emit_fc1_group(0, 9)
        for it in range(10):
            emit_cscale(0, it)
        if dbg == "cb":
            dbg_done(cb[0][:], wait=(s_s, cb_ready[0]))

        for e_idx in range(E):
            for it in (range(10, KT) if e_idx == 0 else range(KT)):
                emit_fc1_group(e_idx, it)
                emit_cscale(e_idx, it)
                if it == KT - 1 and e_idx + 1 < E:
                    cb_ready[(e_idx + 1) % 2] = emit_cb(
                        e_idx + 1, pA[0][:], (pa_free_sem[0], pa_free[0]),
                        extra_evac_wait=(s_v, xg_inc[KT - 1]))
                    pa_free[0] = cb_ready[(e_idx + 1) % 2]
                    pa_free_sem[0] = s_s
            if dbg == "fc1" and e_idx == 0:
                dbg_done(xg[:], cast=True, wait=(s_v, xg_inc[KT - 1]))
            for ot in range(KT):
                par = ot % 2
                first = True
                if e_idx == 0:
                    for h in range(2):
                        mm(pB[par][:, h * 512:(h + 1) * 512],
                           b2bf[0:E, ot * 128:(ot + 1) * 128],
                           combTbf[0:E, h * 512:(h + 1) * 512],
                           start=True, stop=False,
                           waits=[(s_v, COMBT), (s_cg, CG_ALL), pb_free[par]])
                    first = False
                for kt in range(KT):
                    for h in range(2):
                        w = [strip_ready[sc["i"]], (s_v, xg_inc[kt]),
                             pb_free[par]]
                        incs = []
                        if kt == KT - 1 and h == 1:
                            incs = [(s_t, 1)]
                        mm(pB[par][:, h * 512:(h + 1) * 512],
                           wbuf[sc["i"] % NBUF][:, kt * 128:(kt + 1) * 128],
                           xg[:, kt * T + h * 512: kt * T + (h + 1) * 512],
                           start=(first and kt == 0), stop=(kt == KT - 1),
                           waits=w, incs=incs)
                grp = s_t.count
                sc["i"] += 1
                last_tile = (e_idx == E - 1 and ot == KT - 1)
                if not last_tile:
                    em.op("vector",
                          lambda e, ot=ot, p=pB[par][:]: e.tensor_tensor(
                              xres[:, ot * T:(ot + 1) * T], xres[:, ot * T:(ot + 1) * T],
                              p, op=ALU.add),
                          waits=[(s_t, grp)], incs=[(s_v, 1)])
                    pb_free[par] = (s_v, s_v.count)
                    acc_inc[ot] = s_v.count
                    issue_strip(free_wait=(s_v, acc_inc[ot]))
                    if e_idx == E - 1:
                        em.op("sync",
                              lambda e, ot=ot: e.dma_start(
                                  out=yt_h[:, ot * T:(ot + 1) * T],
                                  in_=xres[:, ot * T:(ot + 1) * T]),
                              waits=[(s_v, acc_inc[ot])], incs=[(s_st, 16)])
                else:
                    for hh in range(2):
                        cols = slice(ot * T + hh * 512, ot * T + (hh + 1) * 512)
                        em.op("vector",
                              lambda e, cols=cols, p=pB[par][:, hh * 512:(hh + 1) * 512]:
                                  e.tensor_tensor(xres[:, cols], xres[:, cols],
                                                  p, op=ALU.add),
                              waits=([(s_t, grp)] if hh == 0 else []),
                              incs=[(s_v, 1)])
                        em.op("sync",
                              lambda e, cols=cols: e.dma_start(
                                  out=yt_h[:, cols], in_=xres[:, cols]),
                              waits=[(s_v, s_v.count)], incs=[(s_st, 16)])
                    acc_inc[ot] = s_v.count

        assert sc["i"] == NSTRIP == next_strip[0]
        em.op("sync", lambda e: e.nop(), waits=[(s_st, s_st.count)])

        @block.sync
        def _(eng):
            em.play(eng, "sync")

        @block.gpsimd
        def _(eng):
            em.play(eng, "gpsimd")

        @block.vector
        def _(eng):
            em.play(eng, "vector")

        @block.scalar
        def _(eng):
            em.play(eng, "scalar")

        @block.tensor
        def _(eng):
            em.play(eng, "tensor")

    return nc


def _stage_inputs(L0_attn_x, L0_attn_Wq, L0_attn_bq, L0_attn_Wkv, L0_attn_bkv,
                  L0_ffn_gate_w, L0_ffn_fc1_w, L0_ffn_fc1_b, L0_ffn_fc2_w,
                  L0_ffn_fc2_b, L0_ffn_alpha):
    bf = ml_dtypes.bfloat16
    x = np.asarray(L0_attn_x, dtype=np.float32).reshape(8192, D)
    xt = x.reshape(NCORES, T, KT, 128).transpose(0, 3, 2, 1)
    xt = np.ascontiguousarray(xt).reshape(NCORES, 128, KT * T)

    Wq = np.asarray(L0_attn_Wq, dtype=np.float32)
    wq = np.ascontiguousarray(
        Wq.reshape(KT, 128, KT, 128).transpose(2, 1, 0, 3)).reshape(KT, 128, D)

    Wkv = np.asarray(L0_attn_Wkv, dtype=np.float32)
    idx = np.empty((2 * G, 128), dtype=np.int64)
    for g in range(G):
        d64 = np.arange(128) % 64
        idx[2 * g] = 128 * g + d64
        idx[2 * g + 1] = 128 * g + 64 + d64
    wkvd = Wkv[:, idx]
    wkvd = np.ascontiguousarray(
        wkvd.reshape(KT, 128, 2 * G, 128).transpose(2, 1, 0, 3)).reshape(2 * G, 128, D)

    W1 = np.asarray(L0_ffn_fc1_w, dtype=np.float32)
    w1 = np.ascontiguousarray(
        W1.reshape(E, KT, 128, KT, 128).transpose(0, 3, 2, 1, 4)).reshape(E, KT, 128, D)
    W2 = np.asarray(L0_ffn_fc2_w, dtype=np.float32)
    w2 = np.ascontiguousarray(
        W2.reshape(E, KT, 128, KT, 128).transpose(0, 3, 2, 1, 4)).reshape(E, KT, 128, D)

    gw = np.asarray(L0_ffn_gate_w, dtype=np.float32)
    gatew = np.ascontiguousarray(gw.reshape(KT, 128, E).transpose(1, 0, 2)).reshape(128, KT * E)

    b1 = np.asarray(L0_ffn_fc1_b, dtype=np.float32)
    b1t = np.ascontiguousarray(b1.reshape(E, KT, 128).transpose(2, 0, 1)).reshape(128, E * KT)
    b2 = np.ascontiguousarray(np.asarray(L0_ffn_fc2_b, dtype=np.float32))
    bqv = np.asarray(L0_attn_bq, dtype=np.float32)
    bqt = np.ascontiguousarray(bqv.reshape(KT, 128).T)
    bkv = np.asarray(L0_attn_bkv, dtype=np.float32)
    bkvd = np.ascontiguousarray(bkv[idx].T)
    alpha = np.ascontiguousarray(
        np.broadcast_to(np.asarray(L0_ffn_alpha, dtype=np.float32)[None, :], (128 * 8, E))).reshape(128, 64)

    p = np.arange(128)
    maskA = np.zeros((128, KT * 32), dtype=bf)
    maskT = np.zeros((32, KT * 128), dtype=bf)
    for qt in range(KT):
        gh = 2 * qt + p // 64
        maskA[p, qt * 32 + gh] = bf(1.0)
        maskT[gh, qt * 128 + p] = bf(1.0)
    onesm = np.ones((128, 128), dtype=bf)
    ident = np.eye(128, dtype=np.float32)
    sel = np.zeros((E, E * 128), dtype=bf)
    for e_idx in range(E):
        sel[e_idx, e_idx * 128:(e_idx + 1) * 128] = 1.0

    common = dict(wq=wq, wkvd=wkvd, w1=w1, w2=w2, gatew=gatew, b1=b1t, b2=b2,
                  bq=bqt, bkvd=bkvd, alpha=alpha, maskA=maskA, maskT=maskT,
                  onesm=onesm, ident=ident, sel=sel)
    return [dict(common, xt=np.ascontiguousarray(xt[c])) for c in range(NCORES)]


def kernel(**inputs):
    in_maps = _stage_inputs(**inputs)
    nc = build_program()
    res = run_bass_kernel_spmd(nc, in_maps, core_ids=list(range(NCORES)))
    parts = []
    for c in range(NCORES):
        yt = np.asarray(res.results[c]["yt"]).reshape(128, KT, T)
        parts.append(yt.transpose(2, 1, 0).reshape(T, D))
    return np.concatenate(parts, axis=0).reshape(1, 8192, D)


# version 22
# speedup vs baseline: 1.0002x; 1.0002x over previous
import numpy as np
import ml_dtypes
import concourse.bass as bass
import concourse.mybir as mybir
from concourse.bass_utils import run_bass_kernel_spmd
from contextlib import ExitStack

F32 = mybir.dt.float32
BF16 = mybir.dt.bfloat16
AF = mybir.ActivationFunctionType
ALU = mybir.AluOpType
AX = mybir.AxisListType

NCORES = 8
D = 2048
T = 1024
KT = D // 128
E = 8
G = 4
NBUF = 7
EPS = 1e-5


class Sem:
    def __init__(self, nc, name):
        self.name = name
        self.h = nc.alloc_semaphore(name=name)
        self.count = 0


class Emitter:

    def __init__(self):
        self.q = {"sync": [], "gpsimd": [], "vector": [], "scalar": [], "tensor": []}
        self.waited = {k: {} for k in self.q}
        self.frozen = False

    def op(self, eng, fn, waits=(), incs=()):
        if self.frozen:
            for s, n in incs:
                s.count += n
            return
        real_waits = []
        for s, v in waits:
            assert v <= s.count, f"wait on future inc: {s.name} {v} > {s.count}"
            if self.waited[eng].get(s.name, -1) < v:
                self.waited[eng][s.name] = v
                real_waits.append((s, v))
        for s, n in incs:
            s.count += n
        self.q[eng].append((fn, real_waits, list(incs)))

    def play(self, engine, name):
        for fn, waits, incs in self.q[name]:
            for s, v in waits:
                engine.wait_ge(s.h, v)
            ins = fn(engine)
            assert len(incs) <= 1, "nop-incs are unsafe (fire before engine-queue drain)"
            for s, n in incs:
                ins.then_inc(s.h, n)
            if name == "vector":
                engine.nop()


def build_program(dbg=None):
    nc = bass.Bass("TRN2", target_bir_lowering=False)

    def din(name, shape, dt=F32):
        return nc.dram_tensor(name, shape, dt, kind="ExternalInput").ap()

    xt_h = din("xt", [128, KT * T])
    wq_h = din("wq", [KT, 128, D])
    wkvd_h = din("wkvd", [2 * G, 128, D])
    w1_h = din("w1", [E, KT, 128, D])
    w2_h = din("w2", [E, KT, 128, D])
    gate_h = din("gatew", [128, KT * E])
    b1_h = din("b1", [128, E * KT])
    b2_h = din("b2", [E, D])
    bq_h = din("bq", [128, KT])
    bkvd_h = din("bkvd", [128, 2 * G])
    alpha_h = din("alpha", [128, 8 * E])
    maskA_h = din("maskA", [128, KT * 32], BF16)
    maskT_h = din("maskT", [32, KT * 128], BF16)
    onesm_h = din("onesm", [128, 128], BF16)
    ident_h = din("ident", [128, 128])
    sel_h = din("sel", [E, E * 128], BF16)
    yt_h = nc.dram_tensor("yt", [128, KT * T], F32, kind="ExternalOutput").ap()

    with ExitStack() as ctx:
        def sb(name, shape, dt=F32):
            return ctx.enter_context(nc.sbuf_tensor(name, shape, dt))

        def psum(name, shape):
            return ctx.enter_context(nc.psum_tensor(name, shape, F32))

        xres = sb("xres", [128, KT * T])
        xln = sb("xln", [128, KT * T], BF16)
        xg = sb("xg", [128, KT * T], BF16)
        wbuf = [sb(f"wbuf{i}", [128, D], BF16) for i in range(NBUF)]
        gbuf = [sb(f"gbuf{i}", [128, T]) for i in range(4)]
        cb = [sb(f"cb{i}", [128, T]) for i in range(2)]
        b2bf = sb("b2bf", [E, D], BF16)
        gatebf = sb("gatebf", [128, KT * E], BF16)
        b1_sb = sb("b1_sb", [128, E * KT])
        bq_sb = sb("bq_sb", [128, KT])
        bkvd_sb = sb("bkvd_sb", [128, 2 * G])
        alpha_sb = sb("alpha_sb", [128, 8 * E])
        maskA_sb = sb("maskA_sb", [128, KT * 32], BF16)
        maskT_sb = sb("maskT_sb", [32, KT * 128], BF16)
        onesm_sb = sb("onesm_sb", [128, 128], BF16)
        ident_sb = sb("ident_sb", [128, 128])
        sel_sb = sb("sel_sb", [E, E * 128], BF16)
        s_sb = gbuf[2][0:32, :]
        d_sb = gbuf[3][0:32, :]
        attn_bf = sb("attn_bf", [32, T], BF16)
        logits = sb("logits", [128, 8 * E])
        comb = sb("comb", [128, 8 * E])
        combT = sb("combT", [E, T])
        combTbf = sb("combTbf", [E, T], BF16)
        gEX = sb("gEX", [128, 64])
        gP8 = sb("gP8", [128, 64])
        gM8 = sb("gM8", [128, 64])
        gPM = sb("gPM", [128, 64])
        gEQ = sb("gEQ", [128, 64])
        gC = sb("gC", [128, 64])
        gROW = sb("gROW", [128, 72])
        vdup = xg[:, 0:G * T]
        kdup = xg[:, G * T:2 * G * T]
        bfA = [xg[:, 8 * T + i * T: 8 * T + (i + 1) * T] for i in range(2)]
        bfB = [xg[:, 10 * T + i * T: 10 * T + (i + 1) * T] for i in range(2)]
        rstd_bf = xg[:, 12 * T:13 * T]
        sub_bf = [xg[:, 13 * T + i * T: 13 * T + (i + 1) * T] for i in range(2)]
        ae = [xg[:, 15 * T:16 * T], xg[:, 4 * T:5 * T]]

        pA = [psum(f"pA{i}", [128, T]) for i in range(2)]
        pB = [psum(f"pB{i}", [128, T]) for i in range(2)]

        s_xc = [Sem(nc, f"s_x{i}") for i in range(KT)]
        s_c = Sem(nc, "s_c")
        s_c0 = Sem(nc, "s_c0")
        s_cg = Sem(nc, "s_cg")
        s_wslot = [Sem(nc, f"s_w{i}") for i in range(NBUF)]
        s_wf = Sem(nc, "s_wf")
        s_t = Sem(nc, "s_t")
        s_s = Sem(nc, "s_s")
        s_v = Sem(nc, "s_v")
        s_g = Sem(nc, "s_g")
        s_st = Sem(nc, "s_st")

        em = Emitter()
        block = ctx.enter_context(nc.Block())

        def xchunk(ft):
            em.op("sync", lambda e, ft=ft: e.dma_start(
                out=xres[:, ft * T:(ft + 1) * T],
                in_=xt_h[:, ft * T:(ft + 1) * T]),
                  incs=[(s_xc[ft], 16)])

        em.op("sync", lambda e: e.dma_start(out=onesm_sb[:], in_=onesm_h[:]),
              incs=[(s_c0, 16)])
        xchunk(0)
        xchunk(1)
        for dst, src in [(bq_sb[:], bq_h[:]), (bkvd_sb[:], bkvd_h[:]),
                         (b1_sb[:], b1_h[:]), (alpha_sb[:], alpha_h[:]),
                         (maskA_sb[:], maskA_h[:]), (maskT_sb[:], maskT_h[:]),
                         (ident_sb[:], ident_h[:]),
                         (sel_sb[:], sel_h[:])]:
            em.op("sync", lambda e, d=dst, s=src: e.dma_start(out=d, in_=s),
                  incs=[(s_c, 16)])
        IN_ALL = s_c.count
        for ft in range(2, KT):
            xchunk(ft)

        em.op("gpsimd", lambda e: e.dma_start(out=gatebf[:], in_=gate_h[:]),
              incs=[(s_cg, 16)])
        em.op("gpsimd", lambda e: e.dma_start(out=b2bf[:], in_=b2_h[:]),
              incs=[(s_cg, 16)])
        CG_ALL = s_cg.count

        strip_srcs = []
        for g in range(G):
            strip_srcs.append(wkvd_h[2 * g])
        for qt in range(KT):
            strip_srcs.append(wq_h[qt])
        for g in range(G):
            strip_srcs.append(wkvd_h[2 * g + 1])
        for e in range(E):
            for it in range(KT):
                strip_srcs.append(w1_h[e, it])
            for ot in range(KT):
                strip_srcs.append(w2_h[e, ot])
        NSTRIP = len(strip_srcs)
        strip_ready = [None] * NSTRIP
        next_strip = [0]

        def issue_strip(free_wait=None):
            s = next_strip[0]
            if s >= NSTRIP:
                return
            next_strip[0] = s + 1
            waits = []
            if s >= NBUF:
                assert free_wait is not None
                waits.append(free_wait)
            if s == 0:
                waits.append((s_xc[KT - 1], 16))
            slot = s % NBUF
            em.op("gpsimd",
                  lambda e, d=wbuf[slot][:], src=strip_srcs[s]: e.dma_start(out=d, in_=src),
                  waits=waits, incs=[(s_wslot[slot], 16)])
            strip_ready[s] = (s_wslot[slot], s_wslot[slot].count)

        for _ in range(NBUF):
            issue_strip()

        def mm(pt, lhsT, rhs, start, stop, waits=(), incs=()):
            em.op("tensor",
                  lambda e, o=pt, l=lhsT, r=rhs, st_=start, sp=stop:
                      e.matmul(o, l, r, start=st_, stop=sp),
                  waits=waits, incs=incs)

        def layernorm(dst_bf, ready):
            copy_inc = [0] * KT
            sq_inc = [0] * KT
            ft_mm = [0] * KT
            for ft in range(KT):
                w = [ready[ft]]
                if ft >= 2:
                    w.append((s_t, ft_mm[ft - 2]))
                em.op("vector",
                      lambda e, d=bfA[ft % 2], s=xres[:, ft * T:(ft + 1) * T]:
                          e.tensor_copy(d, s),
                      waits=w, incs=[(s_v, 1)])
                copy_inc[ft] = s_v.count
                em.op("scalar",
                      lambda e, d=bfB[ft % 2], s=xres[:, ft * T:(ft + 1) * T]:
                          e.square(d, s),
                      waits=w, incs=[(s_s, 1)])
                sq_inc[ft] = s_s.count
                for h in range(2):
                    mm(pB[0][:, h * 512:(h + 1) * 512], onesm_sb[:],
                       bfA[ft % 2][:, h * 512:(h + 1) * 512],
                       start=(ft == 0), stop=(ft == KT - 1),
                       waits=[(s_v, copy_inc[ft]), (s_c0, 16)])
                for h in range(2):
                    inc = [(s_t, 1)] if (h == 1) else []
                    mm(pB[1][:, h * 512:(h + 1) * 512], onesm_sb[:],
                       bfB[ft % 2][:, h * 512:(h + 1) * 512],
                       start=(ft == 0), stop=(ft == KT - 1),
                       waits=[(s_s, sq_inc[ft])], incs=inc)
                ft_mm[ft] = s_t.count
            em.op("vector", lambda e: e.tensor_copy(gbuf[0][:], pB[0][:]),
                  waits=[(s_t, ft_mm[KT - 1])])
            em.op("scalar", lambda e: e.copy(gbuf[1][:], pB[1][:]),
                  waits=[(s_t, ft_mm[KT - 1])], incs=[(s_s, 1)])
            stats_evac = s_s.count
            em.op("vector", lambda e: e.tensor_scalar(cb[0][:], gbuf[0][:], 1.0 / D, None, op0=ALU.mult),
                  incs=[(s_v, 1)])
            m_done = s_v.count
            rstd_h = [0, 0]
            for hh in range(2):
                c = slice(hh * 512, (hh + 1) * 512)
                em.op("vector",
                      lambda e, c=c: e.tensor_scalar(gbuf[1][:, c], gbuf[1][:, c], 1.0 / D, EPS, op0=ALU.mult, op1=ALU.add),
                      waits=([(s_s, stats_evac)] if hh == 0 else []))
                em.op("vector",
                      lambda e, c=c: e.tensor_tensor(gbuf[0][:, c], cb[0][:, c], cb[0][:, c], op=ALU.mult))
                em.op("vector",
                      lambda e, c=c: e.tensor_tensor(gbuf[1][:, c], gbuf[1][:, c], gbuf[0][:, c], op=ALU.subtract),
                      incs=[(s_v, 1)])
                vh = s_v.count
                em.op("scalar",
                      lambda e, c=c: e.activation(gbuf[0][:, c], gbuf[1][:, c], AF.Ln, bias=0.0, scale=1.0),
                      waits=[(s_v, vh)])
                em.op("scalar",
                      lambda e, c=c: e.activation(rstd_bf[:, c], gbuf[0][:, c], AF.Exp, bias=0.0, scale=-0.5),
                      incs=[(s_s, 1)])
                rstd_h[hh] = s_s.count
            rstd_done = rstd_h[1]
            apply_inc = [0] * KT
            h0_inc = 0
            for ft in range(KT):
                w = [(s_v, m_done), ready[ft]]
                if ft >= 2:
                    w.append((s_v, apply_inc[ft - 2]))
                em.op("vector",
                      lambda e, ft=ft: e.tensor_tensor(
                          sub_bf[ft % 2], xres[:, ft * T:(ft + 1) * T],
                          cb[0][:], op=ALU.subtract),
                      waits=w)
                if ft == 0:
                    em.op("vector",
                          lambda e, d=dst_bf: e.tensor_tensor(
                              d[:, 0:512], sub_bf[0][:, 0:512],
                              rstd_bf[:, 0:512], op=ALU.mult),
                          waits=[(s_s, rstd_h[0])], incs=[(s_v, 1)])
                    h0_inc = s_v.count
                    em.op("vector",
                          lambda e, d=dst_bf: e.tensor_tensor(
                              d[:, 512:1024], sub_bf[0][:, 512:1024],
                              rstd_bf[:, 512:1024], op=ALU.mult),
                          waits=[(s_s, rstd_h[1])], incs=[(s_v, 1)])
                else:
                    em.op("vector",
                          lambda e, ft=ft, d=dst_bf: e.tensor_tensor(
                              d[:, ft * T:(ft + 1) * T], sub_bf[ft % 2], rstd_bf,
                              op=ALU.mult),
                          waits=[(s_s, rstd_done)], incs=[(s_v, 1)])
                apply_inc[ft] = s_v.count
            return apply_inc, h0_inc

        def dbg_done(ap, cast=False, wait=None):
            q = "gpsimd" if cast else "sync"
            em.op(q, lambda e, a=ap: e.dma_start(out=yt_h[0:ap.shape[0], 0:ap.shape[-1]], in_=a),
                  waits=[wait] if wait else [], incs=[(s_st, 16)])
            em.op("sync", lambda e: e.nop(), waits=[(s_st, 16)])
            em.frozen = True

        if dbg == "io":
            em.op("sync", lambda e: e.dma_start(out=yt_h[:], in_=xres[:]),
                  waits=[(s_xc[i], 16) for i in range(KT)], incs=[(s_st, 16)])
            em.op("sync", lambda e: e.nop(), waits=[(s_st, 16)])
            em.frozen = True

        ln1_inc, ln1_h0 = layernorm(xln, [(s_xc[ft], 16) for ft in range(KT)])
        LN1_ALL = ln1_inc[KT - 1]
        if dbg == "ln1":
            dbg_done(xln[:], cast=True, wait=(s_v, LN1_ALL))

        pa_free = [0, 0]
        pa_free_sem = [s_s, s_s]
        strip_i = 0
        nproj = [0]

        def emit_kv_group(g, is_v, par):
            nonlocal strip_i
            for kt in range(KT):
                for h in range(2):
                    w = [strip_ready[strip_i], (s_v, ln1_inc[kt])]
                    if nproj[0] >= 2:
                        w.append((pa_free_sem[par], pa_free[par]))
                    incs = []
                    if kt == KT - 1 and h == 1:
                        incs = [(s_t, 1)]
                    mm(pA[par][:, h * 512:(h + 1) * 512],
                       wbuf[strip_i % NBUF][:, kt * 128:(kt + 1) * 128],
                       xln[:, kt * T + h * 512: kt * T + (h + 1) * 512],
                       start=(kt == 0), stop=(kt == KT - 1),
                       waits=w, incs=incs)
            grp = s_t.count
            strip_i += 1
            nproj[0] += 1
            dst = (vdup if is_v else kdup)[:, g * T:(g + 1) * T]
            em.op("scalar",
                  lambda e, d=dst, p=pA[par][:], c=bkvd_sb[:, 2 * g + is_v:2 * g + is_v + 1]:
                      e.activation(d, p, AF.Identity, bias=c, scale=1.0),
                  waits=[(s_t, grp), (s_c, IN_ALL)], incs=[(s_s, 1)])
            pa_free[par] = s_s.count
            pa_free_sem[par] = s_s
            issue_strip(free_wait=(s_s, s_s.count))
            return s_s.count

        def emit_kv_quad():
            nonlocal strip_i
            s4 = [strip_i + i for i in range(4)]
            slots = [pA[0], pA[1], pB[0], pB[1]]
            for kt in range(KT):
                for gp in range(4):
                    for h in range(2):
                        lnw = ln1_h0 if (kt == 0 and h == 0) else ln1_inc[kt]
                        w = [strip_ready[s4[gp]], (s_v, lnw)]
                        incs = []
                        if kt == KT - 1 and h == 1:
                            incs = [(s_t, 1)]
                        mm(slots[gp][:, h * 512:(h + 1) * 512],
                           wbuf[s4[gp] % NBUF][:, kt * 128:(kt + 1) * 128],
                           xln[:, kt * T + h * 512: kt * T + (h + 1) * 512],
                           start=(kt == 0), stop=(kt == KT - 1),
                           waits=w, incs=incs)
            grps = [s_t.count - 3 + i for i in range(4)]
            strip_i += 4
            nproj[0] += 4
            evs = [0] * 4
            for gp in range(4):
                em.op("scalar",
                      lambda e, d=kdup[:, gp * T:(gp + 1) * T], p=slots[gp][:],
                             c=bkvd_sb[:, 2 * gp:2 * gp + 1]:
                          e.activation(d, p, AF.Identity, bias=c, scale=1.0),
                      waits=[(s_t, grps[gp]), (s_c, IN_ALL)], incs=[(s_s, 1)])
                evs[gp] = s_s.count
                issue_strip(free_wait=(s_s, s_s.count))
            pa_free[0], pa_free[1] = evs[0], evs[1]
            pa_free_sem[0] = pa_free_sem[1] = s_s
            return evs

        kv_evs = emit_kv_quad()

        kdup_ready = s_s.count
        if dbg == "kv":
            dbg_done(xg[:, 0:2 * G * T], cast=True, wait=(s_s, kdup_ready))
        pv_inc = [0] * KT
        scores_grp = [0] * KT
        qslot = [pA[0], pA[1], pB[1]]
        qfree = [(pa_free_sem[0], pa_free[0]), (pa_free_sem[1], pa_free[1]),
                 (s_s, kv_evs[3])]
        for qt in range(KT):
            par = (qt + 1) % 3
            g = qt // 4
            for kt in range(KT):
                for h in range(2):
                    w = [strip_ready[strip_i], (s_v, ln1_inc[kt])]
                    w.append(qfree[par])
                    incs = []
                    if kt == KT - 1 and h == 1:
                        incs = [(s_t, 1)]
                    mm(qslot[par][:, h * 512:(h + 1) * 512],
                       wbuf[strip_i % NBUF][:, kt * 128:(kt + 1) * 128],
                       xln[:, kt * T + h * 512: kt * T + (h + 1) * 512],
                       start=(kt == 0), stop=(kt == KT - 1),
                       waits=w, incs=incs)
            grp = s_t.count
            strip_i += 1

            def emit_scores(j):
                for h in range(2):
                    incs = [(s_t, 1)] if h == 1 else []
                    mm(pB[0][0:32, h * 512:(h + 1) * 512],
                       maskA_sb[:, j * 32:(j + 1) * 32],
                       bfB[j % 2][:, h * 512:(h + 1) * 512],
                       start=(j == 0), stop=(j == KT - 1),
                       waits=[(s_v, pv_inc[j])], incs=incs)
                scores_grp[j] = s_t.count
            if qt >= 1:
                emit_scores(qt - 1)

            em.op("scalar",
                  lambda e, d=bfA[qt % 2][:], p=qslot[par][:], c=bq_sb[:, qt:qt + 1]:
                      e.activation(d, p, AF.Identity, bias=c, scale=1.0),
                  waits=[(s_t, grp)], incs=[(s_s, 1)])
            qfree[par] = (s_s, s_s.count)
            qev = s_s.count
            issue_strip(free_wait=(s_s, qev))
            pw = [(s_s, qev)]
            if qt >= 2:
                pw.append((s_t, scores_grp[qt - 2]))
            em.op("vector",
                  lambda e, d=bfB[qt % 2][:], q=bfA[qt % 2][:], k=kdup[:, g * T:(g + 1) * T]:
                      e.tensor_tensor(d, q, k, op=ALU.mult),
                  waits=pw, incs=[(s_v, 1)])
            pv_inc[qt] = s_v.count
        pa_free = [qfree[0][1], qfree[1][1]]
        pa_free_sem = [qfree[0][0], qfree[1][0]]
        vdup_ev = [0] * G
        vdup_ev[0] = emit_kv_group(0, 1, 0)
        emit_scores(KT - 1)
        SCORES = s_t.count

        em.op("vector", lambda e: e.tensor_scalar(s_sb[:], pB[0][0:32, :], 1.0 / 8.0, None, op0=ALU.mult),
              waits=[(s_t, SCORES)], incs=[(s_v, 1)])
        if dbg == "scores":
            dbg_done(s_sb[:], wait=(s_v, s_v.count))
        em.op("vector", lambda e: e.tensor_tensor(d_sb[:], s_sb[:], s_sb[:], op=ALU.subtract),
              incs=[(s_v, 1)])
        sm_sub = s_v.count
        em.op("scalar", lambda e: e.activation(s_sb[:], d_sb[:], AF.Exp, bias=0.0, scale=1.0),
              waits=[(s_v, sm_sub)])
        em.op("scalar", lambda e: e.activation(d_sb[:], d_sb[:], AF.Exp, bias=0.0, scale=-1.0),
              incs=[(s_s, 1)])
        sm_exp = s_s.count
        em.op("vector", lambda e: e.tensor_tensor(attn_bf[:], s_sb[:], d_sb[:], op=ALU.mult),
              waits=[(s_s, sm_exp)], incs=[(s_v, 1)])
        ATTN = s_v.count

        epi_inc = [0] * KT

        def emit_epilogue(ft):
            par = ft % 2
            g = ft // 4
            w = [(s_v, ATTN)]
            w.append((pa_free_sem[par], pa_free[par]))
            for h in range(2):
                incs = [(s_t, 1)] if h == 1 else []
                mm(pA[par][:, h * 512:(h + 1) * 512],
                   maskT_sb[0:32, ft * 128:(ft + 1) * 128],
                   attn_bf[0:32, h * 512:(h + 1) * 512],
                   start=True, stop=True,
                   waits=w, incs=incs)
            grp = s_t.count
            ew = [(s_t, grp)]
            if ft >= 2:
                ew.append((s_v, epi_inc[ft - 2]))
            em.op("scalar",
                  lambda e, d=ae[par], p=pA[par][:]: e.copy(d, p),
                  waits=ew, incs=[(s_s, 1)])
            pa_free[par] = s_s.count
            pa_free_sem[par] = s_s
            aev = s_s.count
            em.op("vector",
                  lambda e, d=ae[par], v=vdup[:, g * T:(g + 1) * T]:
                      e.tensor_tensor(d, d, v, op=ALU.mult),
                  waits=[(s_s, aev), (s_s, vdup_ev[g])])
            em.op("vector",
                  lambda e, ft=ft, scr=ae[par]: e.tensor_tensor(
                      xres[:, ft * T:(ft + 1) * T], xres[:, ft * T:(ft + 1) * T],
                      scr, op=ALU.add),
                  incs=[(s_v, 1)])
            epi_inc[ft] = s_v.count

        vdup_ev[1] = emit_kv_group(1, 1, 1)
        for ft in range(4):
            emit_epilogue(ft)
        vdup_ev[2] = emit_kv_group(2, 1, 0)
        vdup_ev[3] = emit_kv_group(3, 1, 1)
        for ft in range(4, KT):
            emit_epilogue(ft)

        if dbg == "attn":
            dbg_done(xres[:], wait=(s_v, epi_inc[KT - 1]))

        ln2_inc, ln2_h0 = layernorm(xln, [(s_v, epi_inc[ft]) for ft in range(KT)])
        LN2_ALL = ln2_inc[KT - 1]

        sc = {"i": strip_i}
        pb_free = [None, None]
        xg_inc = [0] * KT
        acc_inc = [0] * KT
        gl_inc = [0] * KT
        cb_ready = [0, 0]

        def emit_fc1_group(e_idx, it):
            par = it % 2
            for kt in range(KT):
                for h in range(2):
                    w = [strip_ready[sc["i"]], (pa_free_sem[par], pa_free[par])]
                    incs = []
                    if kt == KT - 1 and h == 1:
                        incs = [(s_t, 1)]
                    mm(pA[par][:, h * 512:(h + 1) * 512],
                       wbuf[sc["i"] % NBUF][:, kt * 128:(kt + 1) * 128],
                       xln[:, kt * T + h * 512: kt * T + (h + 1) * 512],
                       start=(kt == 0), stop=(kt == KT - 1),
                       waits=w, incs=incs)
            grp = s_t.count
            sc["i"] += 1
            em.op("scalar",
                  lambda e, it=it, p=pA[par][:],
                         c=b1_sb[:, e_idx * KT + it: e_idx * KT + it + 1]:
                      e.activation(xg[:, it * T:(it + 1) * T], p, AF.Gelu, bias=c, scale=1.0),
                  waits=[(s_t, grp)], incs=[(s_s, 1)])
            pa_free[par] = s_s.count
            pa_free_sem[par] = s_s
            gl_inc[it] = s_s.count
            issue_strip(free_wait=(s_s, gl_inc[it]))

        def emit_cscale(e_idx, it):
            em.op("vector",
                  lambda e, it=it, c=cb[e_idx % 2][:]:
                      e.tensor_tensor(xg[:, it * T:(it + 1) * T],
                                      xg[:, it * T:(it + 1) * T], c, op=ALU.mult),
                  waits=[(s_s, gl_inc[it]), (s_s, cb_ready[e_idx % 2])], incs=[(s_v, 1)])
            xg_inc[it] = s_v.count


        def emit_fc1_quad():
            s4 = [sc["i"] + i for i in range(4)]
            slots = [pA[0], pA[1], pB[0], pB[1]]
            for kt in range(KT):
                for itp in range(4):
                    for h in range(2):
                        lnw = ln2_h0 if (kt == 0 and h == 0) else ln2_inc[kt]
                        w = [strip_ready[s4[itp]], (s_v, lnw)]
                        if itp < 2:
                            w.append((pa_free_sem[itp], pa_free[itp]))
                        incs = []
                        if kt == KT - 1 and h == 1:
                            incs = [(s_t, 1)]
                        mm(slots[itp][:, h * 512:(h + 1) * 512],
                           wbuf[s4[itp] % NBUF][:, kt * 128:(kt + 1) * 128],
                           xln[:, kt * T + h * 512: kt * T + (h + 1) * 512],
                           start=(kt == 0), stop=(kt == KT - 1),
                           waits=w, incs=incs)
            grps = [s_t.count - 3 + i for i in range(4)]
            sc["i"] += 4
            for itp in range(4):
                em.op("scalar",
                      lambda e, it=itp, p=slots[itp][:],
                             c=b1_sb[:, itp:itp + 1]:
                          e.activation(xg[:, it * T:(it + 1) * T], p, AF.Gelu, bias=c, scale=1.0),
                      waits=[(s_t, grps[itp])], incs=[(s_s, 1)])
                gl_inc[itp] = s_s.count
                issue_strip(free_wait=(s_s, gl_inc[itp]))
            pa_free[0], pa_free[1] = gl_inc[0], gl_inc[1]
            pa_free_sem[0] = pa_free_sem[1] = s_s

        emit_fc1_quad()

        emit_fc1_group(0, 4)
        for h in range(2):
            for kt in range(KT):
                w = [(s_cg, CG_ALL), (s_v, ln2_inc[kt])]
                if h == 0 and kt == 0:
                    w.append((s_s, gl_inc[3]))
                incs = [(s_t, 1)] if kt == KT - 1 else []
                mm(pB[1][0:8, h * 512:(h + 1) * 512],
                   gatebf[:, kt * 8:(kt + 1) * 8],
                   xln[:, kt * T + h * 512: kt * T + (h + 1) * 512],
                   start=(kt == 0), stop=(kt == KT - 1),
                   waits=w, incs=incs)
        GATE_MM = s_t.count
        ltT = gbuf[2][0:8, :]
        em.op("scalar", lambda e, d=ltT: e.copy(d, pB[1][0:8, :]),
              waits=[(s_t, GATE_MM)], incs=[(s_s, 1)])
        LTT_EV = s_s.count

        emit_fc1_group(0, 5)

        for tt in range(8):
            incs = [(s_t, 1)] if tt == 7 else []
            mm(pB[0][:, tt * 8:(tt + 1) * 8],
               ltT[:, tt * 128:(tt + 1) * 128],
               ident_sb[0:8, 0:8],
               start=True, stop=True,
               waits=[(s_s, LTT_EV)], incs=incs)
        LTR_MM = s_t.count
        em.op("scalar", lambda e: e.copy(logits[:, 0:64], pB[0][:, 0:64]),
              waits=[(s_t, LTR_MM)], incs=[(s_s, 1)])
        LOGITS_EV = s_s.count

        emit_fc1_group(0, 6)

        def T8(tt):
            return slice(tt * 8, (tt + 1) * 8)

        def C1(base, tt):
            return gROW[:, base + tt: base + tt + 1]

        NM, S2c, WS, RWS = 0, 8, 16, 24

        def vstep(fns, waits=()):
            for i, fn in enumerate(fns):
                em.op("vector", fn, waits=(waits if i == 0 else ()),
                      incs=([(s_v, 1)] if i == len(fns) - 1 else []))
            em.op("vector", lambda e: e.engine_nop(), waits=[(s_v, s_v.count)])
            return s_v.count

        vstep([(lambda e, tt=tt: e.reduce_max(C1(NM, tt), logits[:, T8(tt)], axis=AX.X, negate=True))
               for tt in range(8)], waits=[(s_s, LOGITS_EV)])
        nm_done = s_v.count
        for tt in range(8):
            em.op("scalar", lambda e, tt=tt: e.activation(
                gEX[:, T8(tt)], logits[:, T8(tt)], AF.Exp, bias=C1(NM, tt), scale=1.0),
                  waits=[(s_v, nm_done)], incs=([(s_s, 1)] if tt == 7 else []))
        ex_done = s_s.count
        vstep([lambda e: e.tensor_scalar(gM8[:], gEX[:], 1.0, 2.0, op0=ALU.is_ge, op1=ALU.mult)],
              waits=[(s_s, ex_done)])
        vstep([lambda e: e.tensor_tensor(gPM[:], gEX[:], gM8[:], op=ALU.subtract)])
        vstep([(lambda e, tt=tt: e.reduce_max(C1(S2c, tt), gPM[:, T8(tt)], axis=AX.X))
               for tt in range(8)])
        vstep([(lambda e, tt=tt: e.tensor_scalar(gM8[:, T8(tt)], gEX[:, T8(tt)], C1(S2c, tt), None, op0=ALU.is_ge))
               for tt in range(8)])
        vstep([lambda e: e.tensor_scalar(gROW[:, WS:WS + 8], gROW[:, S2c:S2c + 8], 1.0, None, op0=ALU.add)])
        vstep([lambda e: e.reciprocal(gROW[:, RWS:RWS + 8], gROW[:, WS:WS + 8])])
        vstep([lambda e: e.tensor_tensor(gC[:], gEX[:], gM8[:], op=ALU.mult)])
        vstep([(lambda e, tt=tt: e.tensor_scalar(gC[:, T8(tt)], gC[:, T8(tt)], C1(RWS, tt), None, op0=ALU.mult))
               for tt in range(8)])
        vstep([lambda e: e.tensor_tensor(comb[:], gC[:], alpha_sb[:], op=ALU.mult)],
              waits=[(s_c, IN_ALL)])

        COMB = s_v.count
        emit_fc1_group(0, 7)
        tr_ev = 0
        for tt in range(8):
            bank = tt % 2
            w = [(s_v, COMB)]
            if tt >= 2:
                w.append((s_s, tr_ev - 1))
            em.op("tensor",
                  lambda e, tt=tt, off=bank * 512: e.transpose(
                      pB[0][0:8, off:off + 128], comb[:, tt * 8:(tt + 1) * 8], ident_sb[:]),
                  waits=w, incs=[(s_t, 1)])
            grp = s_t.count
            em.op("scalar",
                  lambda e, tt=tt, off=bank * 512: e.copy(
                      combT[0:8, tt * 128:(tt + 1) * 128], pB[0][0:8, off:off + 128]),
                  waits=[(s_t, grp)], incs=[(s_s, 1)])
            tr_ev = s_s.count
        em.op("vector", lambda e: e.tensor_copy(combTbf[:], combT[:]),
              waits=[(s_s, tr_ev)], incs=[(s_v, 1)])
        COMBT = s_v.count
        if dbg == "gate":
            dbg_done(comb[:], wait=(s_v, COMB))
        if dbg == "combT":
            dbg_done(combT[:], wait=(s_s, tr_ev))
        if dbg == "logits":
            dbg_done(logits[:], wait=(s_s, LOGITS_EV))

        emit_fc1_group(0, 8)

        def emit_cb(e_idx, pt, free_wait, extra_evac_wait=None):
            w = [(s_v, COMBT), free_wait]
            for h in range(2):
                incs = [(s_t, 1)] if h == 1 else []
                mm(pt[:, h * 512:(h + 1) * 512],
                   sel_sb[0:E, e_idx * 128:(e_idx + 1) * 128],
                   combTbf[0:E, h * 512:(h + 1) * 512],
                   start=True, stop=True, waits=w, incs=incs)
            grp = s_t.count
            ew = [(s_t, grp)]
            if extra_evac_wait is not None:
                ew.append(extra_evac_wait)
            em.op("scalar",
                  lambda e, d=cb[e_idx % 2][:], p=pt[:]: e.copy(d, p),
                  waits=ew, incs=[(s_s, 1)])
            return s_s.count

        cb_ready[0] = emit_cb(0, pB[0][:], (s_s, tr_ev))
        pb_free[0] = (s_s, cb_ready[0])
        pb_free[1] = (s_s, LTT_EV)
        emit_fc1_group(0, 9)
        for it in range(10):
            emit_cscale(0, it)
        if dbg == "cb":
            dbg_done(cb[0][:], wait=(s_s, cb_ready[0]))

        for e_idx in range(E):
            for it in (range(10, KT) if e_idx == 0 else range(KT)):
                emit_fc1_group(e_idx, it)
                emit_cscale(e_idx, it)
                if it == KT - 1 and e_idx + 1 < E:
                    cb_ready[(e_idx + 1) % 2] = emit_cb(
                        e_idx + 1, pA[0][:], (pa_free_sem[0], pa_free[0]),
                        extra_evac_wait=(s_v, xg_inc[KT - 1]))
                    pa_free[0] = cb_ready[(e_idx + 1) % 2]
                    pa_free_sem[0] = s_s
            if dbg == "fc1" and e_idx == 0:
                dbg_done(xg[:], cast=True, wait=(s_v, xg_inc[KT - 1]))
            for ot in range(KT):
                par = ot % 2
                first = True
                if e_idx == 0:
                    for h in range(2):
                        mm(pB[par][:, h * 512:(h + 1) * 512],
                           b2bf[0:E, ot * 128:(ot + 1) * 128],
                           combTbf[0:E, h * 512:(h + 1) * 512],
                           start=True, stop=False,
                           waits=[(s_v, COMBT), (s_cg, CG_ALL), pb_free[par]])
                    first = False
                for kt in range(KT):
                    for h in range(2):
                        w = [strip_ready[sc["i"]], (s_v, xg_inc[kt]),
                             pb_free[par]]
                        incs = []
                        if kt == KT - 1 and h == 1:
                            incs = [(s_t, 1)]
                        mm(pB[par][:, h * 512:(h + 1) * 512],
                           wbuf[sc["i"] % NBUF][:, kt * 128:(kt + 1) * 128],
                           xg[:, kt * T + h * 512: kt * T + (h + 1) * 512],
                           start=(first and kt == 0), stop=(kt == KT - 1),
                           waits=w, incs=incs)
                grp = s_t.count
                sc["i"] += 1
                last_tile = (e_idx == E - 1 and ot == KT - 1)
                if not last_tile:
                    em.op("vector",
                          lambda e, ot=ot, p=pB[par][:]: e.tensor_tensor(
                              xres[:, ot * T:(ot + 1) * T], xres[:, ot * T:(ot + 1) * T],
                              p, op=ALU.add),
                          waits=[(s_t, grp)], incs=[(s_v, 1)])
                    pb_free[par] = (s_v, s_v.count)
                    acc_inc[ot] = s_v.count
                    issue_strip(free_wait=(s_v, acc_inc[ot]))
                    if e_idx == E - 1:
                        em.op("sync",
                              lambda e, ot=ot: e.dma_start(
                                  out=yt_h[:, ot * T:(ot + 1) * T],
                                  in_=xres[:, ot * T:(ot + 1) * T]),
                              waits=[(s_v, acc_inc[ot])], incs=[(s_st, 16)])
                else:
                    for hh in range(2):
                        cols = slice(ot * T + hh * 512, ot * T + (hh + 1) * 512)
                        em.op("vector",
                              lambda e, cols=cols, p=pB[par][:, hh * 512:(hh + 1) * 512]:
                                  e.tensor_tensor(xres[:, cols], xres[:, cols],
                                                  p, op=ALU.add),
                              waits=([(s_t, grp)] if hh == 0 else []),
                              incs=[(s_v, 1)])
                        em.op("sync",
                              lambda e, cols=cols: e.dma_start(
                                  out=yt_h[:, cols], in_=xres[:, cols]),
                              waits=[(s_v, s_v.count)], incs=[(s_st, 16)])
                    acc_inc[ot] = s_v.count

        assert sc["i"] == NSTRIP == next_strip[0]
        em.op("sync", lambda e: e.nop(), waits=[(s_st, s_st.count)])

        @block.sync
        def _(eng):
            em.play(eng, "sync")

        @block.gpsimd
        def _(eng):
            em.play(eng, "gpsimd")

        @block.vector
        def _(eng):
            em.play(eng, "vector")

        @block.scalar
        def _(eng):
            em.play(eng, "scalar")

        @block.tensor
        def _(eng):
            em.play(eng, "tensor")

    return nc


def _stage_inputs(L0_attn_x, L0_attn_Wq, L0_attn_bq, L0_attn_Wkv, L0_attn_bkv,
                  L0_ffn_gate_w, L0_ffn_fc1_w, L0_ffn_fc1_b, L0_ffn_fc2_w,
                  L0_ffn_fc2_b, L0_ffn_alpha):
    bf = ml_dtypes.bfloat16
    x = np.asarray(L0_attn_x, dtype=np.float32).reshape(8192, D)
    xt = x.reshape(NCORES, T, KT, 128).transpose(0, 3, 2, 1)
    xt = np.ascontiguousarray(xt).reshape(NCORES, 128, KT * T)

    Wq = np.asarray(L0_attn_Wq, dtype=np.float32)
    wq = np.ascontiguousarray(
        Wq.reshape(KT, 128, KT, 128).transpose(2, 1, 0, 3)).reshape(KT, 128, D)

    Wkv = np.asarray(L0_attn_Wkv, dtype=np.float32)
    idx = np.empty((2 * G, 128), dtype=np.int64)
    for g in range(G):
        d64 = np.arange(128) % 64
        idx[2 * g] = 128 * g + d64
        idx[2 * g + 1] = 128 * g + 64 + d64
    wkvd = Wkv[:, idx]
    wkvd = np.ascontiguousarray(
        wkvd.reshape(KT, 128, 2 * G, 128).transpose(2, 1, 0, 3)).reshape(2 * G, 128, D)

    W1 = np.asarray(L0_ffn_fc1_w, dtype=np.float32)
    w1 = np.ascontiguousarray(
        W1.reshape(E, KT, 128, KT, 128).transpose(0, 3, 2, 1, 4)).reshape(E, KT, 128, D)
    W2 = np.asarray(L0_ffn_fc2_w, dtype=np.float32)
    w2 = np.ascontiguousarray(
        W2.reshape(E, KT, 128, KT, 128).transpose(0, 3, 2, 1, 4)).reshape(E, KT, 128, D)

    gw = np.asarray(L0_ffn_gate_w, dtype=np.float32)
    gatew = np.ascontiguousarray(gw.reshape(KT, 128, E).transpose(1, 0, 2)).reshape(128, KT * E)

    b1 = np.asarray(L0_ffn_fc1_b, dtype=np.float32)
    b1t = np.ascontiguousarray(b1.reshape(E, KT, 128).transpose(2, 0, 1)).reshape(128, E * KT)
    b2 = np.ascontiguousarray(np.asarray(L0_ffn_fc2_b, dtype=np.float32))
    bqv = np.asarray(L0_attn_bq, dtype=np.float32)
    bqt = np.ascontiguousarray(bqv.reshape(KT, 128).T)
    bkv = np.asarray(L0_attn_bkv, dtype=np.float32)
    bkvd = np.ascontiguousarray(bkv[idx].T)
    alpha = np.ascontiguousarray(
        np.broadcast_to(np.asarray(L0_ffn_alpha, dtype=np.float32)[None, :], (128 * 8, E))).reshape(128, 64)

    p = np.arange(128)
    maskA = np.zeros((128, KT * 32), dtype=bf)
    maskT = np.zeros((32, KT * 128), dtype=bf)
    for qt in range(KT):
        gh = 2 * qt + p // 64
        maskA[p, qt * 32 + gh] = bf(1.0)
        maskT[gh, qt * 128 + p] = bf(1.0)
    onesm = np.ones((128, 128), dtype=bf)
    ident = np.eye(128, dtype=np.float32)
    sel = np.zeros((E, E * 128), dtype=bf)
    for e_idx in range(E):
        sel[e_idx, e_idx * 128:(e_idx + 1) * 128] = 1.0

    common = dict(wq=wq, wkvd=wkvd, w1=w1, w2=w2, gatew=gatew, b1=b1t, b2=b2,
                  bq=bqt, bkvd=bkvd, alpha=alpha, maskA=maskA, maskT=maskT,
                  onesm=onesm, ident=ident, sel=sel)
    return [dict(common, xt=np.ascontiguousarray(xt[c])) for c in range(NCORES)]


def kernel(**inputs):
    in_maps = _stage_inputs(**inputs)
    nc = build_program()
    res = run_bass_kernel_spmd(nc, in_maps, core_ids=list(range(NCORES)))
    parts = []
    for c in range(NCORES):
        yt = np.asarray(res.results[c]["yt"]).reshape(128, KT, T)
        parts.append(yt.transpose(2, 1, 0).reshape(T, D))
    return np.concatenate(parts, axis=0).reshape(1, 8192, D)


# version 23
# speedup vs baseline: 1.0019x; 1.0017x over previous
import numpy as np
import ml_dtypes
import concourse.bass as bass
import concourse.mybir as mybir
from concourse.bass_utils import run_bass_kernel_spmd
from contextlib import ExitStack

F32 = mybir.dt.float32
BF16 = mybir.dt.bfloat16
AF = mybir.ActivationFunctionType
ALU = mybir.AluOpType
AX = mybir.AxisListType

NCORES = 8
D = 2048
T = 1024
KT = D // 128
E = 8
G = 4
NBUF = 6
EPS = 1e-5


class Sem:
    def __init__(self, nc, name):
        self.name = name
        self.h = nc.alloc_semaphore(name=name)
        self.count = 0


class Emitter:

    def __init__(self):
        self.q = {"sync": [], "gpsimd": [], "vector": [], "scalar": [], "tensor": []}
        self.waited = {k: {} for k in self.q}
        self.frozen = False

    def op(self, eng, fn, waits=(), incs=()):
        if self.frozen:
            for s, n in incs:
                s.count += n
            return
        real_waits = []
        for s, v in waits:
            assert v <= s.count, f"wait on future inc: {s.name} {v} > {s.count}"
            if self.waited[eng].get(s.name, -1) < v:
                self.waited[eng][s.name] = v
                real_waits.append((s, v))
        for s, n in incs:
            s.count += n
        self.q[eng].append((fn, real_waits, list(incs)))

    def play(self, engine, name):
        for fn, waits, incs in self.q[name]:
            for s, v in waits:
                engine.wait_ge(s.h, v)
            ins = fn(engine)
            assert len(incs) <= 1, "nop-incs are unsafe (fire before engine-queue drain)"
            for s, n in incs:
                ins.then_inc(s.h, n)
            if name == "vector":
                engine.nop()


def build_program(dbg=None):
    nc = bass.Bass("TRN2", target_bir_lowering=False)

    def din(name, shape, dt=F32):
        return nc.dram_tensor(name, shape, dt, kind="ExternalInput").ap()

    xt_h = din("xt", [128, KT * T])
    wq_h = din("wq", [KT, 128, D])
    wkvd_h = din("wkvd", [2 * G, 128, D])
    w1_h = din("w1", [E, KT, 128, D])
    w2_h = din("w2", [E, KT, 128, D])
    gate_h = din("gatew", [128, KT * E])
    b1_h = din("b1", [128, E * KT])
    b2_h = din("b2", [E, D])
    bq_h = din("bq", [128, KT])
    bkvd_h = din("bkvd", [128, 2 * G])
    alpha_h = din("alpha", [128, 8 * E])
    maskA_h = din("maskA", [128, KT * 32], BF16)
    maskT_h = din("maskT", [32, KT * 128], BF16)
    onesm_h = din("onesm", [128, 128], BF16)
    ident_h = din("ident", [128, 128])
    sel_h = din("sel", [E, E * 128], BF16)
    yt_h = nc.dram_tensor("yt", [128, KT * T], F32, kind="ExternalOutput").ap()

    with ExitStack() as ctx:
        def sb(name, shape, dt=F32):
            return ctx.enter_context(nc.sbuf_tensor(name, shape, dt))

        def psum(name, shape):
            return ctx.enter_context(nc.psum_tensor(name, shape, F32))

        xres = sb("xres", [128, KT * T])
        xln = sb("xln", [128, KT * T], BF16)
        xg = sb("xg", [128, KT * T], BF16)
        wbuf = [sb(f"wbuf{i}", [128, D], BF16) for i in range(NBUF)]
        gbuf = [sb(f"gbuf{i}", [128, T]) for i in range(4)]
        cb = [sb(f"cb{i}", [128, T]) for i in range(2)]
        b2bf = sb("b2bf", [E, D], BF16)
        gatebf = sb("gatebf", [128, KT * E], BF16)
        b1_sb = sb("b1_sb", [128, E * KT])
        bq_sb = sb("bq_sb", [128, KT])
        bkvd_sb = sb("bkvd_sb", [128, 2 * G])
        alpha_sb = sb("alpha_sb", [128, 8 * E])
        maskA_sb = sb("maskA_sb", [128, KT * 32], BF16)
        maskT_sb = sb("maskT_sb", [32, KT * 128], BF16)
        onesm_sb = sb("onesm_sb", [128, 128], BF16)
        ident_sb = sb("ident_sb", [128, 128])
        sel_sb = sb("sel_sb", [E, E * 128], BF16)
        s_sb = gbuf[2][0:32, :]
        d_sb = gbuf[3][0:32, :]
        attn_bf = sb("attn_bf", [32, T], BF16)
        logits = sb("logits", [128, 8 * E])
        comb = sb("comb", [128, 8 * E])
        combT = sb("combT", [E, T])
        combTbf = sb("combTbf", [E, T], BF16)
        gEX = sb("gEX", [128, 64])
        gP8 = sb("gP8", [128, 64])
        gM8 = sb("gM8", [128, 64])
        gPM = sb("gPM", [128, 64])
        gEQ = sb("gEQ", [128, 64])
        gC = sb("gC", [128, 64])
        gROW = sb("gROW", [128, 72])
        vdup = xg[:, 0:G * T]
        kdup = xg[:, G * T:2 * G * T]
        bfA = [xg[:, 8 * T + i * T: 8 * T + (i + 1) * T] for i in range(2)]
        bfB = [xg[:, 10 * T + i * T: 10 * T + (i + 1) * T] for i in range(2)]
        rstd_bf = xg[:, 12 * T:13 * T]
        sub_bf = [xg[:, 13 * T + i * T: 13 * T + (i + 1) * T] for i in range(2)]
        ae = [xg[:, 15 * T:16 * T], xg[:, 4 * T:5 * T]]

        pA = [psum(f"pA{i}", [128, T]) for i in range(2)]
        pB = [psum(f"pB{i}", [128, T]) for i in range(2)]

        s_xc = [Sem(nc, f"s_x{i}") for i in range(KT)]
        s_c = Sem(nc, "s_c")
        s_cg = Sem(nc, "s_cg")
        s_wslot = [Sem(nc, f"s_w{i}") for i in range(NBUF)]
        s_wf = Sem(nc, "s_wf")
        s_t = Sem(nc, "s_t")
        s_s = Sem(nc, "s_s")
        s_v = Sem(nc, "s_v")
        s_g = Sem(nc, "s_g")
        s_st = Sem(nc, "s_st")

        em = Emitter()
        block = ctx.enter_context(nc.Block())

        def xchunk(ft):
            em.op("sync", lambda e, ft=ft: e.dma_start(
                out=xres[:, ft * T:(ft + 1) * T],
                in_=xt_h[:, ft * T:(ft + 1) * T]),
                  incs=[(s_xc[ft], 16)])

        xchunk(0)
        xchunk(1)
        for dst, src in [(bq_sb[:], bq_h[:]), (bkvd_sb[:], bkvd_h[:]),
                         (b1_sb[:], b1_h[:]), (alpha_sb[:], alpha_h[:]),
                         (maskA_sb[:], maskA_h[:]), (maskT_sb[:], maskT_h[:]),
                         (onesm_sb[:], onesm_h[:]), (ident_sb[:], ident_h[:]),
                         (sel_sb[:], sel_h[:])]:
            em.op("sync", lambda e, d=dst, s=src: e.dma_start(out=d, in_=s),
                  incs=[(s_c, 16)])
        IN_ALL = s_c.count
        for ft in range(2, KT):
            xchunk(ft)

        em.op("gpsimd", lambda e: e.dma_start(out=gatebf[:], in_=gate_h[:]),
              incs=[(s_cg, 16)])
        em.op("gpsimd", lambda e: e.dma_start(out=b2bf[:], in_=b2_h[:]),
              incs=[(s_cg, 16)])
        CG_ALL = s_cg.count

        strip_srcs = []
        for g in range(G):
            strip_srcs.append(wkvd_h[2 * g])
        for qt in range(KT):
            strip_srcs.append(wq_h[qt])
        for g in range(G):
            strip_srcs.append(wkvd_h[2 * g + 1])
        for e in range(E):
            for it in range(KT):
                strip_srcs.append(w1_h[e, it])
            for ot in range(KT):
                strip_srcs.append(w2_h[e, ot])
        NSTRIP = len(strip_srcs)
        strip_ready = [None] * NSTRIP
        next_strip = [0]

        def issue_strip(free_wait=None):
            s = next_strip[0]
            if s >= NSTRIP:
                return
            next_strip[0] = s + 1
            waits = []
            if s >= NBUF:
                assert free_wait is not None
                waits.append(free_wait)
            if s == 0:
                waits.append((s_xc[KT - 1], 16))
            slot = s % NBUF
            em.op("gpsimd",
                  lambda e, d=wbuf[slot][:], src=strip_srcs[s]: e.dma_start(out=d, in_=src),
                  waits=waits, incs=[(s_wslot[slot], 16)])
            strip_ready[s] = (s_wslot[slot], s_wslot[slot].count)

        for _ in range(NBUF):
            issue_strip()

        def mm(pt, lhsT, rhs, start, stop, waits=(), incs=()):
            em.op("tensor",
                  lambda e, o=pt, l=lhsT, r=rhs, st_=start, sp=stop:
                      e.matmul(o, l, r, start=st_, stop=sp),
                  waits=waits, incs=incs)

        def layernorm(dst_bf, ready):
            copy_inc = [0] * KT
            sq_inc = [0] * KT
            ft_mm = [0] * KT
            for ft in range(KT):
                w = [ready[ft]]
                if ft >= 2:
                    w.append((s_t, ft_mm[ft - 2]))
                em.op("vector",
                      lambda e, d=bfA[ft % 2], s=xres[:, ft * T:(ft + 1) * T]:
                          e.tensor_copy(d, s),
                      waits=w, incs=[(s_v, 1)])
                copy_inc[ft] = s_v.count
                em.op("scalar",
                      lambda e, d=bfB[ft % 2], s=xres[:, ft * T:(ft + 1) * T]:
                          e.square(d, s),
                      waits=w, incs=[(s_s, 1)])
                sq_inc[ft] = s_s.count
                for h in range(2):
                    mm(pB[0][:, h * 512:(h + 1) * 512], onesm_sb[:],
                       bfA[ft % 2][:, h * 512:(h + 1) * 512],
                       start=(ft == 0), stop=(ft == KT - 1),
                       waits=[(s_v, copy_inc[ft]), (s_c, IN_ALL)])
                for h in range(2):
                    inc = [(s_t, 1)] if (h == 1) else []
                    mm(pB[1][:, h * 512:(h + 1) * 512], onesm_sb[:],
                       bfB[ft % 2][:, h * 512:(h + 1) * 512],
                       start=(ft == 0), stop=(ft == KT - 1),
                       waits=[(s_s, sq_inc[ft])], incs=inc)
                ft_mm[ft] = s_t.count
            em.op("scalar", lambda e: e.copy(gbuf[0][:], pB[0][:]),
                  waits=[(s_t, ft_mm[KT - 1])])
            em.op("scalar", lambda e: e.copy(gbuf[1][:], pB[1][:]),
                  incs=[(s_s, 1)])
            stats_evac = s_s.count
            em.op("vector", lambda e: e.tensor_scalar(cb[0][:], gbuf[0][:], 1.0 / D, None, op0=ALU.mult),
                  waits=[(s_s, stats_evac)], incs=[(s_v, 1)])
            m_done = s_v.count
            em.op("vector", lambda e: e.tensor_scalar(gbuf[1][:], gbuf[1][:], 1.0 / D, EPS, op0=ALU.mult, op1=ALU.add))
            em.op("vector", lambda e: e.tensor_tensor(gbuf[0][:], cb[0][:], cb[0][:], op=ALU.mult))
            em.op("vector", lambda e: e.tensor_tensor(gbuf[1][:], gbuf[1][:], gbuf[0][:], op=ALU.subtract),
                  incs=[(s_v, 1)])
            var_done = s_v.count
            em.op("scalar", lambda e: e.activation(gbuf[0][:], gbuf[1][:], AF.Ln, bias=0.0, scale=1.0),
                  waits=[(s_v, var_done)])
            em.op("scalar", lambda e: e.activation(rstd_bf, gbuf[0][:], AF.Exp, bias=0.0, scale=-0.5),
                  incs=[(s_s, 1)])
            rstd_done = s_s.count
            apply_inc = [0] * KT
            for ft in range(KT):
                w = [(s_v, m_done), ready[ft]]
                if ft >= 2:
                    w.append((s_v, apply_inc[ft - 2]))
                em.op("vector",
                      lambda e, ft=ft: e.tensor_tensor(
                          sub_bf[ft % 2], xres[:, ft * T:(ft + 1) * T],
                          cb[0][:], op=ALU.subtract),
                      waits=w)
                em.op("vector",
                      lambda e, ft=ft, d=dst_bf: e.tensor_tensor(
                          d[:, ft * T:(ft + 1) * T], sub_bf[ft % 2], rstd_bf,
                          op=ALU.mult),
                      waits=[(s_s, rstd_done)], incs=[(s_v, 1)])
                apply_inc[ft] = s_v.count
            return apply_inc

        def dbg_done(ap, cast=False, wait=None):
            q = "gpsimd" if cast else "sync"
            em.op(q, lambda e, a=ap: e.dma_start(out=yt_h[0:ap.shape[0], 0:ap.shape[-1]], in_=a),
                  waits=[wait] if wait else [], incs=[(s_st, 16)])
            em.op("sync", lambda e: e.nop(), waits=[(s_st, 16)])
            em.frozen = True

        if dbg == "io":
            em.op("sync", lambda e: e.dma_start(out=yt_h[:], in_=xres[:]),
                  waits=[(s_xc[i], 16) for i in range(KT)], incs=[(s_st, 16)])
            em.op("sync", lambda e: e.nop(), waits=[(s_st, 16)])
            em.frozen = True

        ln1_inc = layernorm(xln, [(s_xc[ft], 16) for ft in range(KT)])
        LN1_ALL = ln1_inc[KT - 1]
        if dbg == "ln1":
            dbg_done(xln[:], cast=True, wait=(s_v, LN1_ALL))

        pa_free = [0, 0]
        pa_free_sem = [s_s, s_s]
        strip_i = 0
        nproj = [0]

        def emit_kv_group(g, is_v, par):
            nonlocal strip_i
            for kt in range(KT):
                for h in range(2):
                    w = [strip_ready[strip_i], (s_v, ln1_inc[kt])]
                    if nproj[0] >= 2:
                        w.append((pa_free_sem[par], pa_free[par]))
                    incs = []
                    if kt == KT - 1 and h == 1:
                        incs = [(s_t, 1)]
                    mm(pA[par][:, h * 512:(h + 1) * 512],
                       wbuf[strip_i % NBUF][:, kt * 128:(kt + 1) * 128],
                       xln[:, kt * T + h * 512: kt * T + (h + 1) * 512],
                       start=(kt == 0), stop=(kt == KT - 1),
                       waits=w, incs=incs)
            grp = s_t.count
            strip_i += 1
            nproj[0] += 1
            dst = (vdup if is_v else kdup)[:, g * T:(g + 1) * T]
            em.op("scalar",
                  lambda e, d=dst, p=pA[par][:], c=bkvd_sb[:, 2 * g + is_v:2 * g + is_v + 1]:
                      e.activation(d, p, AF.Identity, bias=c, scale=1.0),
                  waits=[(s_t, grp), (s_c, IN_ALL)], incs=[(s_s, 1)])
            pa_free[par] = s_s.count
            pa_free_sem[par] = s_s
            issue_strip(free_wait=(s_s, s_s.count))
            return s_s.count

        def emit_kv_quad():
            nonlocal strip_i
            s4 = [strip_i + i for i in range(4)]
            slots = [pA[0], pA[1], pB[0], pB[1]]
            for kt in range(KT):
                for gp in range(4):
                    for h in range(2):
                        w = [strip_ready[s4[gp]], (s_v, ln1_inc[kt])]
                        incs = []
                        if kt == KT - 1 and h == 1:
                            incs = [(s_t, 1)]
                        mm(slots[gp][:, h * 512:(h + 1) * 512],
                           wbuf[s4[gp] % NBUF][:, kt * 128:(kt + 1) * 128],
                           xln[:, kt * T + h * 512: kt * T + (h + 1) * 512],
                           start=(kt == 0), stop=(kt == KT - 1),
                           waits=w, incs=incs)
            grps = [s_t.count - 3 + i for i in range(4)]
            strip_i += 4
            nproj[0] += 4
            evs = [0] * 4
            for gp in range(4):
                em.op("scalar",
                      lambda e, d=kdup[:, gp * T:(gp + 1) * T], p=slots[gp][:],
                             c=bkvd_sb[:, 2 * gp:2 * gp + 1]:
                          e.activation(d, p, AF.Identity, bias=c, scale=1.0),
                      waits=[(s_t, grps[gp]), (s_c, IN_ALL)], incs=[(s_s, 1)])
                evs[gp] = s_s.count
                issue_strip(free_wait=(s_s, s_s.count))
            pa_free[0], pa_free[1] = evs[0], evs[1]
            pa_free_sem[0] = pa_free_sem[1] = s_s
            return evs

        kv_evs = emit_kv_quad()

        kdup_ready = s_s.count
        if dbg == "kv":
            dbg_done(xg[:, 0:2 * G * T], cast=True, wait=(s_s, kdup_ready))
        pv_inc = [0] * KT
        scores_grp = [0] * KT
        qslot = [pA[0], pA[1], pB[1]]
        qfree = [(pa_free_sem[0], pa_free[0]), (pa_free_sem[1], pa_free[1]),
                 (s_s, kv_evs[3])]
        for qt in range(KT):
            par = qt % 3
            g = qt // 4
            for kt in range(KT):
                for h in range(2):
                    w = [strip_ready[strip_i], (s_v, ln1_inc[kt])]
                    w.append(qfree[par])
                    incs = []
                    if kt == KT - 1 and h == 1:
                        incs = [(s_t, 1)]
                    mm(qslot[par][:, h * 512:(h + 1) * 512],
                       wbuf[strip_i % NBUF][:, kt * 128:(kt + 1) * 128],
                       xln[:, kt * T + h * 512: kt * T + (h + 1) * 512],
                       start=(kt == 0), stop=(kt == KT - 1),
                       waits=w, incs=incs)
            grp = s_t.count
            strip_i += 1

            def emit_scores(j):
                for h in range(2):
                    incs = [(s_t, 1)] if h == 1 else []
                    mm(pB[0][0:32, h * 512:(h + 1) * 512],
                       maskA_sb[:, j * 32:(j + 1) * 32],
                       bfB[j % 2][:, h * 512:(h + 1) * 512],
                       start=(j == 0), stop=(j == KT - 1),
                       waits=[(s_v, pv_inc[j])], incs=incs)
                scores_grp[j] = s_t.count
            if qt >= 1:
                emit_scores(qt - 1)

            em.op("scalar",
                  lambda e, d=bfA[qt % 2][:], p=qslot[par][:], c=bq_sb[:, qt:qt + 1]:
                      e.activation(d, p, AF.Identity, bias=c, scale=1.0),
                  waits=[(s_t, grp)], incs=[(s_s, 1)])
            qfree[par] = (s_s, s_s.count)
            qev = s_s.count
            issue_strip(free_wait=(s_s, qev))
            pw = [(s_s, qev)]
            if qt >= 2:
                pw.append((s_t, scores_grp[qt - 2]))
            em.op("vector",
                  lambda e, d=bfB[qt % 2][:], q=bfA[qt % 2][:], k=kdup[:, g * T:(g + 1) * T]:
                      e.tensor_tensor(d, q, k, op=ALU.mult),
                  waits=pw, incs=[(s_v, 1)])
            pv_inc[qt] = s_v.count
        pa_free = [qfree[0][1], qfree[1][1]]
        pa_free_sem = [qfree[0][0], qfree[1][0]]
        vdup_ev = [0] * G
        vdup_ev[0] = emit_kv_group(0, 1, 0)
        emit_scores(KT - 1)
        SCORES = s_t.count

        em.op("vector", lambda e: e.tensor_scalar(s_sb[:], pB[0][0:32, :], 1.0 / 8.0, None, op0=ALU.mult),
              waits=[(s_t, SCORES)], incs=[(s_v, 1)])
        if dbg == "scores":
            dbg_done(s_sb[:], wait=(s_v, s_v.count))
        em.op("vector", lambda e: e.tensor_tensor(d_sb[:], s_sb[:], s_sb[:], op=ALU.subtract),
              incs=[(s_v, 1)])
        sm_sub = s_v.count
        em.op("scalar", lambda e: e.activation(s_sb[:], d_sb[:], AF.Exp, bias=0.0, scale=1.0),
              waits=[(s_v, sm_sub)])
        em.op("scalar", lambda e: e.activation(d_sb[:], d_sb[:], AF.Exp, bias=0.0, scale=-1.0),
              incs=[(s_s, 1)])
        sm_exp = s_s.count
        em.op("vector", lambda e: e.tensor_tensor(attn_bf[:], s_sb[:], d_sb[:], op=ALU.mult),
              waits=[(s_s, sm_exp)], incs=[(s_v, 1)])
        ATTN = s_v.count

        epi_inc = [0] * KT

        def emit_epilogue(ft):
            par = ft % 2
            g = ft // 4
            w = [(s_v, ATTN)]
            w.append((pa_free_sem[par], pa_free[par]))
            for h in range(2):
                incs = [(s_t, 1)] if h == 1 else []
                mm(pA[par][:, h * 512:(h + 1) * 512],
                   maskT_sb[0:32, ft * 128:(ft + 1) * 128],
                   attn_bf[0:32, h * 512:(h + 1) * 512],
                   start=True, stop=True,
                   waits=w, incs=incs)
            grp = s_t.count
            ew = [(s_t, grp)]
            if ft >= 2:
                ew.append((s_v, epi_inc[ft - 2]))
            em.op("scalar",
                  lambda e, d=ae[par], p=pA[par][:]: e.copy(d, p),
                  waits=ew, incs=[(s_s, 1)])
            pa_free[par] = s_s.count
            pa_free_sem[par] = s_s
            aev = s_s.count
            em.op("vector",
                  lambda e, d=ae[par], v=vdup[:, g * T:(g + 1) * T]:
                      e.tensor_tensor(d, d, v, op=ALU.mult),
                  waits=[(s_s, aev), (s_s, vdup_ev[g])])
            em.op("vector",
                  lambda e, ft=ft, scr=ae[par]: e.tensor_tensor(
                      xres[:, ft * T:(ft + 1) * T], xres[:, ft * T:(ft + 1) * T],
                      scr, op=ALU.add),
                  incs=[(s_v, 1)])
            epi_inc[ft] = s_v.count

        vdup_ev[1] = emit_kv_group(1, 1, 1)
        for ft in range(4):
            emit_epilogue(ft)
        vdup_ev[2] = emit_kv_group(2, 1, 0)
        vdup_ev[3] = emit_kv_group(3, 1, 1)
        for ft in range(4, KT):
            emit_epilogue(ft)

        if dbg == "attn":
            dbg_done(xres[:], wait=(s_v, epi_inc[KT - 1]))

        ln2_inc = layernorm(xln, [(s_v, epi_inc[ft]) for ft in range(KT)])
        LN2_ALL = ln2_inc[KT - 1]

        sc = {"i": strip_i}
        pb_free = [None, None]
        xg_inc = [0] * KT
        acc_inc = [0] * KT
        gl_inc = [0] * KT
        cb_ready = [0, 0]

        def emit_fc1_group(e_idx, it):
            par = it % 2
            for kt in range(KT):
                for h in range(2):
                    w = [strip_ready[sc["i"]], (pa_free_sem[par], pa_free[par])]
                    incs = []
                    if kt == KT - 1 and h == 1:
                        incs = [(s_t, 1)]
                    mm(pA[par][:, h * 512:(h + 1) * 512],
                       wbuf[sc["i"] % NBUF][:, kt * 128:(kt + 1) * 128],
                       xln[:, kt * T + h * 512: kt * T + (h + 1) * 512],
                       start=(kt == 0), stop=(kt == KT - 1),
                       waits=w, incs=incs)
            grp = s_t.count
            sc["i"] += 1
            em.op("scalar",
                  lambda e, it=it, p=pA[par][:],
                         c=b1_sb[:, e_idx * KT + it: e_idx * KT + it + 1]:
                      e.activation(xg[:, it * T:(it + 1) * T], p, AF.Gelu, bias=c, scale=1.0),
                  waits=[(s_t, grp)], incs=[(s_s, 1)])
            pa_free[par] = s_s.count
            pa_free_sem[par] = s_s
            gl_inc[it] = s_s.count
            issue_strip(free_wait=(s_s, gl_inc[it]))

        def emit_cscale(e_idx, it):
            em.op("vector",
                  lambda e, it=it, c=cb[e_idx % 2][:]:
                      e.tensor_tensor(xg[:, it * T:(it + 1) * T],
                                      xg[:, it * T:(it + 1) * T], c, op=ALU.mult),
                  waits=[(s_s, gl_inc[it]), (s_s, cb_ready[e_idx % 2])], incs=[(s_v, 1)])
            xg_inc[it] = s_v.count


        def emit_fc1_quad():
            s4 = [sc["i"] + i for i in range(4)]
            slots = [pA[0], pA[1], pB[0], pB[1]]
            for kt in range(KT):
                for itp in range(4):
                    for h in range(2):
                        w = [strip_ready[s4[itp]], (s_v, ln2_inc[kt])]
                        if itp < 2:
                            w.append((pa_free_sem[itp], pa_free[itp]))
                        incs = []
                        if kt == KT - 1 and h == 1:
                            incs = [(s_t, 1)]
                        mm(slots[itp][:, h * 512:(h + 1) * 512],
                           wbuf[s4[itp] % NBUF][:, kt * 128:(kt + 1) * 128],
                           xln[:, kt * T + h * 512: kt * T + (h + 1) * 512],
                           start=(kt == 0), stop=(kt == KT - 1),
                           waits=w, incs=incs)
            grps = [s_t.count - 3 + i for i in range(4)]
            sc["i"] += 4
            for itp in (3, 2, 1, 0):
                em.op("scalar",
                      lambda e, it=itp, p=slots[itp][:],
                             c=b1_sb[:, itp:itp + 1]:
                          e.activation(xg[:, it * T:(it + 1) * T], p, AF.Gelu, bias=c, scale=1.0),
                      waits=[(s_t, grps[itp])], incs=[(s_s, 1)])
                gl_inc[itp] = s_s.count
                issue_strip(free_wait=(s_s, gl_inc[itp]))
            pa_free[0], pa_free[1] = gl_inc[0], gl_inc[1]
            pa_free_sem[0] = pa_free_sem[1] = s_s

        emit_fc1_quad()

        for h in range(2):
            for kt in range(KT):
                w = [(s_cg, CG_ALL), (s_v, ln2_inc[kt])]
                if h == 0 and kt == 0:
                    w.append((s_s, gl_inc[3]))
                incs = [(s_t, 1)] if kt == KT - 1 else []
                mm(pB[1][0:8, h * 512:(h + 1) * 512],
                   gatebf[:, kt * 8:(kt + 1) * 8],
                   xln[:, kt * T + h * 512: kt * T + (h + 1) * 512],
                   start=(kt == 0), stop=(kt == KT - 1),
                   waits=w, incs=incs)
        GATE_MM = s_t.count
        ltT = gbuf[2][0:8, :]
        em.op("scalar", lambda e, d=ltT: e.copy(d, pB[1][0:8, :]),
              waits=[(s_t, GATE_MM)], incs=[(s_s, 1)])
        LTT_EV = s_s.count

        emit_fc1_group(0, 4)

        for tt in range(8):
            incs = [(s_t, 1)] if tt == 7 else []
            mm(pB[0][:, tt * 8:(tt + 1) * 8],
               ltT[:, tt * 128:(tt + 1) * 128],
               ident_sb[0:8, 0:8],
               start=True, stop=True,
               waits=[(s_s, LTT_EV)], incs=incs)
        LTR_MM = s_t.count
        em.op("scalar", lambda e: e.copy(logits[:, 0:64], pB[0][:, 0:64]),
              waits=[(s_t, LTR_MM)], incs=[(s_s, 1)])
        LOGITS_EV = s_s.count

        emit_fc1_group(0, 5)

        def T8(tt):
            return slice(tt * 8, (tt + 1) * 8)

        def C1(base, tt):
            return gROW[:, base + tt: base + tt + 1]

        NM, S2c, WS, RWS = 0, 8, 16, 24

        def vstep(fns, waits=()):
            for i, fn in enumerate(fns):
                em.op("vector", fn, waits=(waits if i == 0 else ()),
                      incs=([(s_v, 1)] if i == len(fns) - 1 else []))
            em.op("vector", lambda e: e.engine_nop(), waits=[(s_v, s_v.count)])
            return s_v.count

        vstep([(lambda e, tt=tt: e.reduce_max(C1(NM, tt), logits[:, T8(tt)], axis=AX.X, negate=True))
               for tt in range(8)], waits=[(s_s, LOGITS_EV)])
        nm_done = s_v.count
        for tt in range(8):
            em.op("scalar", lambda e, tt=tt: e.activation(
                gEX[:, T8(tt)], logits[:, T8(tt)], AF.Exp, bias=C1(NM, tt), scale=1.0),
                  waits=[(s_v, nm_done)], incs=([(s_s, 1)] if tt == 7 else []))
        ex_done = s_s.count
        vstep([lambda e: e.tensor_scalar(gM8[:], gEX[:], 1.0, 2.0, op0=ALU.is_ge, op1=ALU.mult)],
              waits=[(s_s, ex_done)])
        vstep([lambda e: e.tensor_tensor(gPM[:], gEX[:], gM8[:], op=ALU.subtract)])
        vstep([(lambda e, tt=tt: e.reduce_max(C1(S2c, tt), gPM[:, T8(tt)], axis=AX.X))
               for tt in range(8)])
        vstep([(lambda e, tt=tt: e.tensor_scalar(gM8[:, T8(tt)], gEX[:, T8(tt)], C1(S2c, tt), None, op0=ALU.is_ge))
               for tt in range(8)])
        vstep([lambda e: e.tensor_scalar(gROW[:, WS:WS + 8], gROW[:, S2c:S2c + 8], 1.0, None, op0=ALU.add)])
        vstep([lambda e: e.reciprocal(gROW[:, RWS:RWS + 8], gROW[:, WS:WS + 8])])
        vstep([lambda e: e.tensor_tensor(gC[:], gEX[:], gM8[:], op=ALU.mult)])
        vstep([(lambda e, tt=tt: e.tensor_scalar(gC[:, T8(tt)], gC[:, T8(tt)], C1(RWS, tt), None, op0=ALU.mult))
               for tt in range(8)])
        vstep([lambda e: e.tensor_tensor(comb[:], gC[:], alpha_sb[:], op=ALU.mult)],
              waits=[(s_c, IN_ALL)])

        COMB = s_v.count
        emit_fc1_group(0, 6)
        emit_fc1_group(0, 7)
        tr_ev = 0
        for tt in range(8):
            bank = tt % 2
            w = [(s_v, COMB)]
            if tt >= 2:
                w.append((s_s, tr_ev - 1))
            em.op("tensor",
                  lambda e, tt=tt, off=bank * 512: e.transpose(
                      pB[0][0:8, off:off + 128], comb[:, tt * 8:(tt + 1) * 8], ident_sb[:]),
                  waits=w, incs=[(s_t, 1)])
            grp = s_t.count
            em.op("scalar",
                  lambda e, tt=tt, off=bank * 512: e.copy(
                      combT[0:8, tt * 128:(tt + 1) * 128], pB[0][0:8, off:off + 128]),
                  waits=[(s_t, grp)], incs=[(s_s, 1)])
            tr_ev = s_s.count
        em.op("vector", lambda e: e.tensor_copy(combTbf[:], combT[:]),
              waits=[(s_s, tr_ev)], incs=[(s_v, 1)])
        COMBT = s_v.count
        if dbg == "gate":
            dbg_done(comb[:], wait=(s_v, COMB))
        if dbg == "combT":
            dbg_done(combT[:], wait=(s_s, tr_ev))
        if dbg == "logits":
            dbg_done(logits[:], wait=(s_s, LOGITS_EV))

        emit_fc1_group(0, 8)

        def emit_cb(e_idx, pt, free_wait, extra_evac_wait=None):
            w = [(s_v, COMBT), free_wait]
            for h in range(2):
                incs = [(s_t, 1)] if h == 1 else []
                mm(pt[:, h * 512:(h + 1) * 512],
                   sel_sb[0:E, e_idx * 128:(e_idx + 1) * 128],
                   combTbf[0:E, h * 512:(h + 1) * 512],
                   start=True, stop=True, waits=w, incs=incs)
            grp = s_t.count
            ew = [(s_t, grp)]
            if extra_evac_wait is not None:
                ew.append(extra_evac_wait)
            em.op("scalar",
                  lambda e, d=cb[e_idx % 2][:], p=pt[:]: e.copy(d, p),
                  waits=ew, incs=[(s_s, 1)])
            return s_s.count

        cb_ready[0] = emit_cb(0, pB[0][:], (s_s, tr_ev))
        pb_free[0] = (s_s, cb_ready[0])
        pb_free[1] = (s_s, LTT_EV)
        emit_fc1_group(0, 9)
        for it in range(10):
            emit_cscale(0, it)
        if dbg == "cb":
            dbg_done(cb[0][:], wait=(s_s, cb_ready[0]))

        for e_idx in range(E):
            for it in (range(10, KT) if e_idx == 0 else range(KT)):
                emit_fc1_group(e_idx, it)
                emit_cscale(e_idx, it)
                if it == KT - 1 and e_idx + 1 < E:
                    cb_ready[(e_idx + 1) % 2] = emit_cb(
                        e_idx + 1, pA[0][:], (pa_free_sem[0], pa_free[0]),
                        extra_evac_wait=(s_v, xg_inc[KT - 1]))
                    pa_free[0] = cb_ready[(e_idx + 1) % 2]
                    pa_free_sem[0] = s_s
            if dbg == "fc1" and e_idx == 0:
                dbg_done(xg[:], cast=True, wait=(s_v, xg_inc[KT - 1]))
            for ot in range(KT):
                par = ot % 2
                first = True
                if e_idx == 0:
                    for h in range(2):
                        mm(pB[par][:, h * 512:(h + 1) * 512],
                           b2bf[0:E, ot * 128:(ot + 1) * 128],
                           combTbf[0:E, h * 512:(h + 1) * 512],
                           start=True, stop=False,
                           waits=[(s_v, COMBT), (s_cg, CG_ALL), pb_free[par]])
                    first = False
                for kt in range(KT):
                    for h in range(2):
                        w = [strip_ready[sc["i"]], (s_v, xg_inc[kt]),
                             pb_free[par]]
                        incs = []
                        if kt == KT - 1 and h == 1:
                            incs = [(s_t, 1)]
                        mm(pB[par][:, h * 512:(h + 1) * 512],
                           wbuf[sc["i"] % NBUF][:, kt * 128:(kt + 1) * 128],
                           xg[:, kt * T + h * 512: kt * T + (h + 1) * 512],
                           start=(first and kt == 0), stop=(kt == KT - 1),
                           waits=w, incs=incs)
                grp = s_t.count
                sc["i"] += 1
                last_tile = (e_idx == E - 1 and ot == KT - 1)
                if not last_tile:
                    em.op("vector",
                          lambda e, ot=ot, p=pB[par][:]: e.tensor_tensor(
                              xres[:, ot * T:(ot + 1) * T], xres[:, ot * T:(ot + 1) * T],
                              p, op=ALU.add),
                          waits=[(s_t, grp)], incs=[(s_v, 1)])
                    pb_free[par] = (s_v, s_v.count)
                    acc_inc[ot] = s_v.count
                    issue_strip(free_wait=(s_v, acc_inc[ot]))
                    if e_idx == E - 1:
                        em.op("sync",
                              lambda e, ot=ot: e.dma_start(
                                  out=yt_h[:, ot * T:(ot + 1) * T],
                                  in_=xres[:, ot * T:(ot + 1) * T]),
                              waits=[(s_v, acc_inc[ot])], incs=[(s_st, 16)])
                else:
                    for hh in range(2):
                        cols = slice(ot * T + hh * 512, ot * T + (hh + 1) * 512)
                        em.op("vector",
                              lambda e, cols=cols, p=pB[par][:, hh * 512:(hh + 1) * 512]:
                                  e.tensor_tensor(xres[:, cols], xres[:, cols],
                                                  p, op=ALU.add),
                              waits=([(s_t, grp)] if hh == 0 else []),
                              incs=[(s_v, 1)])
                        em.op("sync",
                              lambda e, cols=cols: e.dma_start(
                                  out=yt_h[:, cols], in_=xres[:, cols]),
                              waits=[(s_v, s_v.count)], incs=[(s_st, 16)])
                    acc_inc[ot] = s_v.count

        assert sc["i"] == NSTRIP == next_strip[0]
        em.op("sync", lambda e: e.nop(), waits=[(s_st, s_st.count)])

        @block.sync
        def _(eng):
            em.play(eng, "sync")

        @block.gpsimd
        def _(eng):
            em.play(eng, "gpsimd")

        @block.vector
        def _(eng):
            em.play(eng, "vector")

        @block.scalar
        def _(eng):
            em.play(eng, "scalar")

        @block.tensor
        def _(eng):
            em.play(eng, "tensor")

    return nc


def _stage_inputs(L0_attn_x, L0_attn_Wq, L0_attn_bq, L0_attn_Wkv, L0_attn_bkv,
                  L0_ffn_gate_w, L0_ffn_fc1_w, L0_ffn_fc1_b, L0_ffn_fc2_w,
                  L0_ffn_fc2_b, L0_ffn_alpha):
    bf = ml_dtypes.bfloat16
    x = np.asarray(L0_attn_x, dtype=np.float32).reshape(8192, D)
    xt = x.reshape(NCORES, T, KT, 128).transpose(0, 3, 2, 1)
    xt = np.ascontiguousarray(xt).reshape(NCORES, 128, KT * T)

    Wq = np.asarray(L0_attn_Wq, dtype=np.float32)
    wq = np.ascontiguousarray(
        Wq.reshape(KT, 128, KT, 128).transpose(2, 1, 0, 3)).reshape(KT, 128, D)

    Wkv = np.asarray(L0_attn_Wkv, dtype=np.float32)
    idx = np.empty((2 * G, 128), dtype=np.int64)
    for g in range(G):
        d64 = np.arange(128) % 64
        idx[2 * g] = 128 * g + d64
        idx[2 * g + 1] = 128 * g + 64 + d64
    wkvd = Wkv[:, idx]
    wkvd = np.ascontiguousarray(
        wkvd.reshape(KT, 128, 2 * G, 128).transpose(2, 1, 0, 3)).reshape(2 * G, 128, D)

    W1 = np.asarray(L0_ffn_fc1_w, dtype=np.float32)
    w1 = np.ascontiguousarray(
        W1.reshape(E, KT, 128, KT, 128).transpose(0, 3, 2, 1, 4)).reshape(E, KT, 128, D)
    W2 = np.asarray(L0_ffn_fc2_w, dtype=np.float32)
    w2 = np.ascontiguousarray(
        W2.reshape(E, KT, 128, KT, 128).transpose(0, 3, 2, 1, 4)).reshape(E, KT, 128, D)

    gw = np.asarray(L0_ffn_gate_w, dtype=np.float32)
    gatew = np.ascontiguousarray(gw.reshape(KT, 128, E).transpose(1, 0, 2)).reshape(128, KT * E)

    b1 = np.asarray(L0_ffn_fc1_b, dtype=np.float32)
    b1t = np.ascontiguousarray(b1.reshape(E, KT, 128).transpose(2, 0, 1)).reshape(128, E * KT)
    b2 = np.ascontiguousarray(np.asarray(L0_ffn_fc2_b, dtype=np.float32))
    bqv = np.asarray(L0_attn_bq, dtype=np.float32)
    bqt = np.ascontiguousarray(bqv.reshape(KT, 128).T)
    bkv = np.asarray(L0_attn_bkv, dtype=np.float32)
    bkvd = np.ascontiguousarray(bkv[idx].T)
    alpha = np.ascontiguousarray(
        np.broadcast_to(np.asarray(L0_ffn_alpha, dtype=np.float32)[None, :], (128 * 8, E))).reshape(128, 64)

    p = np.arange(128)
    maskA = np.zeros((128, KT * 32), dtype=bf)
    maskT = np.zeros((32, KT * 128), dtype=bf)
    for qt in range(KT):
        gh = 2 * qt + p // 64
        maskA[p, qt * 32 + gh] = bf(1.0)
        maskT[gh, qt * 128 + p] = bf(1.0)
    onesm = np.ones((128, 128), dtype=bf)
    ident = np.eye(128, dtype=np.float32)
    sel = np.zeros((E, E * 128), dtype=bf)
    for e_idx in range(E):
        sel[e_idx, e_idx * 128:(e_idx + 1) * 128] = 1.0

    common = dict(wq=wq, wkvd=wkvd, w1=w1, w2=w2, gatew=gatew, b1=b1t, b2=b2,
                  bq=bqt, bkvd=bkvd, alpha=alpha, maskA=maskA, maskT=maskT,
                  onesm=onesm, ident=ident, sel=sel)
    return [dict(common, xt=np.ascontiguousarray(xt[c])) for c in range(NCORES)]


def kernel(**inputs):
    in_maps = _stage_inputs(**inputs)
    nc = build_program()
    res = run_bass_kernel_spmd(nc, in_maps, core_ids=list(range(NCORES)))
    parts = []
    for c in range(NCORES):
        yt = np.asarray(res.results[c]["yt"]).reshape(128, KT, T)
        parts.append(yt.transpose(2, 1, 0).reshape(T, D))
    return np.concatenate(parts, axis=0).reshape(1, 8192, D)
